# version 07
# baseline (speedup 1.0000x reference)
import math
from contextlib import ExitStack

import numpy as np
import ml_dtypes

import concourse.bass as bass
import concourse.mybir as mybir
from concourse.bass_utils import run_bass_kernel_spmd

BF16 = mybir.dt.bfloat16
F32 = mybir.dt.float32
I32 = mybir.dt.int32
AF = mybir.ActivationFunctionType
ALU = mybir.AluOpType
AXL = mybir.AxisListType
bfloat16 = ml_dtypes.bfloat16

T = 2048
DM = 4096
NIC = 86
II = 11008
SCALE = 1.0 / math.sqrt(128.0)
EPS = 1e-5
ENGINES = ("sync", "gpsimd", "tensor", "scalar", "vector")


class Ctr:
    __slots__ = ("v",)

    def __init__(self):
        self.v = 0


class Builder:
    def __init__(self, nc):
        self.nc = nc
        self.streams = {k: [] for k in ENGINES}

    def op(self, eng, fn):
        self.streams[eng].append(fn)

    def wait(self, eng, sem, val):
        if val:
            self.streams[eng].append(lambda e, s=sem, v=val: e.wait_ge(s, v))

    def flush(self):
        nc = self.nc
        streams = self.streams
        with nc.Block() as blk:
            for name in ENGINES:
                ops = streams[name]

                def runner(e, ops=ops):
                    for f in ops:
                        f(e)

                getattr(blk, name)(runner)
        self.streams = {k: [] for k in ENGINES}


def build_nc(dump=0):
    nc = bass.Bass("TRN2", target_bir_lowering=False)

    x_d = nc.dram_tensor("x_d", [T, DM], F32, kind="ExternalInput").ap()
    wqt_d = nc.dram_tensor("wqt_d", [32, 128, 32, 128], F32, kind="ExternalInput").ap()
    wkt_d = nc.dram_tensor("wkt_d", [128, 32, 1024], F32, kind="ExternalInput").ap()
    wvt_d = nc.dram_tensor("wvt_d", [128, 32, 1024], F32, kind="ExternalInput").ap()
    wot_d = nc.dram_tensor("wot_d", [32, 128, 32, 128], F32, kind="ExternalInput").ap()
    qg_d = nc.dram_tensor("qg_d", [II, DM], I32, kind="ExternalInput").ap()
    qu_d = nc.dram_tensor("qu_d", [II, DM], I32, kind="ExternalInput").ap()
    sg_d = nc.dram_tensor("sg_d", [II, 32], F32, kind="ExternalInput").ap()
    zg_d = nc.dram_tensor("zg_d", [II, 32], F32, kind="ExternalInput").ap()
    su_d = nc.dram_tensor("su_d", [II, 32], F32, kind="ExternalInput").ap()
    zu_d = nc.dram_tensor("zu_d", [II, 32], F32, kind="ExternalInput").ap()
    qdt_d = nc.dram_tensor("qdt_d", [NIC, 128, 32, 128], I32, kind="ExternalInput").ap()
    sdt_d = nc.dram_tensor("sdt_d", [128, 32, NIC], F32, kind="ExternalInput").ap()
    zdt_d = nc.dram_tensor("zdt_d", [128, 32, NIC], F32, kind="ExternalInput").ap()
    ident_d = nc.dram_tensor("ident_d", [128, 128], BF16, kind="ExternalInput").ap()
    onec_d = nc.dram_tensor("onec_d", [128, 1], BF16, kind="ExternalInput").ap()
    oner_d = nc.dram_tensor("oner_d", [1, 128], F32, kind="ExternalInput").ap()
    out_d = nc.dram_tensor("out_d", [T, DM], F32, kind="ExternalOutput").ap()

    xfm_sp = nc.dram_tensor("xfm_sp", [2, 128, 32, 1024], BF16).ap()
    k_sp = nc.dram_tensor("k_sp", [128, 8, 2048], BF16).ap()
    v_sp = nc.dram_tensor("v_sp", [128, 16, 1024], BF16).ap()
    r1_sp = nc.dram_tensor("r1_sp", [T, DM], BF16).ap()
    h_sp = nc.dram_tensor("h_sp", [II, T], BF16).ap()
    wdt_sp = nc.dram_tensor("wdt_sp", [II, DM], BF16).ap()
    dbg = {}
    if dump:
        dbg["r1"] = nc.dram_tensor("dbg_r1", [T, DM], F32, kind="ExternalOutput").ap()
    if dump >= 2:
        dbg["h"] = nc.dram_tensor("dbg_h", [II, 512], F32, kind="ExternalOutput").ap()
        dbg["wdt"] = nc.dram_tensor("dbg_wdt", [512, DM], F32, kind="ExternalOutput").ap()

    es = ExitStack()
    with es:
        NDMA = 10
        dsem = [es.enter_context(nc.semaphore(f"d{i}")) for i in range(NDMA)]
        dctr = [Ctr() for _ in range(NDMA)]
        sP = es.enter_context(nc.semaphore("sP"))
        sA = es.enter_context(nc.semaphore("sA"))
        sV = es.enter_context(nc.semaphore("sV"))
        cP, cA, cV = Ctr(), Ctr(), Ctr()
        SEM = {"tensor": (sP, cP), "scalar": (sA, cA), "vector": (sV, cV)}

        ident = es.enter_context(nc.sbuf_tensor("ident", [128, 128], BF16))
        onec = es.enter_context(nc.sbuf_tensor("onec", [128, 1], BF16))
        oner = es.enter_context(nc.sbuf_tensor("oner", [1, 128], F32))

        b = Builder(nc)
        W = b.wait

        def dma(eng, ch, out, in_):
            s, c = dsem[ch], dctr[ch]
            c.v += 16
            b.op(eng, lambda e, o=out, i=in_, s=s: e.dma_start(out=o, in_=i).then_inc(s, 16))
            return (s, c.v)

        def WD(eng, tok):
            if tok:
                b.wait(eng, tok[0], tok[1])

        def emit(eng, fn, inc=False):
            s, c = SEM[eng]
            if inc:
                c.v += 1
                b.op(eng, lambda e, f=fn, s=s: f(e).then_inc(s, 1))
                return c.v
            b.op(eng, fn)
            return None

        def drain():
            for i in range(NDMA):
                if dctr[i].v:
                    W("gpsimd", dsem[i], dctr[i].v)
                    W("sync", dsem[i], dctr[i].v)

        dma("sync", 0, ident[:, :], ident_d)
        dma("sync", 0, onec[:, :], onec_d)
        c_consts = dma("sync", 0, oner[:, :], oner_d)
        for eng in ("tensor", "vector", "scalar", "gpsimd"):
            WD(eng, c_consts)
        drain(); b.flush()

        with ExitStack() as esA:
            wk_bf = esA.enter_context(nc.sbuf_tensor("wk_bf", [128, 32, 1024], BF16))
            xfm = esA.enter_context(nc.sbuf_tensor("xfmA", [128, 32, 1024], BF16))
            xb = [esA.enter_context(nc.sbuf_tensor(f"xbA{i}", [128, DM], BF16)) for i in range(2)]
            kst = [esA.enter_context(nc.sbuf_tensor(f"kstA{i}", [128, 8, 512], BF16)) for i in range(2)]
            ptb = [esA.enter_context(nc.psum_tensor(f"ptbA{i}", [128, 512], F32)) for i in range(2)]
            pk = [esA.enter_context(nc.psum_tensor(f"pkA{i}", [128, 512], F32)) for i in range(2)]

            c_wk = None

            evacB = {}
            mmT_last = {}
            k_evac = {}
            k_spill = {}
            kmm = {}
            xfm_spill = {}
            bidx = 0
            for H in range(2):
                for tt in range(8):
                    tg = H * 8 + tt
                    if tg >= 2:
                        W("gpsimd", sP, mmT_last[tg - 2])
                    c_x = dma("gpsimd", 2 + tg % 2, xb[tg % 2][:, :], x_d[tg * 128:(tg + 1) * 128, :])
                    if tg == 1:
                        c_wk = dma("gpsimd", 1, wk_bf[:, :, :], wkt_d)
                    WD("tensor", c_x)
                    for cb in range(8):
                        if bidx >= 2:
                            W("tensor", sV, evacB[bidx - 2])
                        for j in range(4):
                            c = cb * 4 + j
                            r = emit("tensor",
                                     lambda e, pb=ptb[bidx % 2], xt=xb[tg % 2], c=c, j=j:
                                     e.matmul(pb[:, j * 128:(j + 1) * 128],
                                              xt[:, c * 128:(c + 1) * 128], ident[:, :],
                                              start=True, stop=True), inc=(j == 3))
                        mmT_last[tg] = r
                        W("vector", sP, r)
                        if tg == 8 and cb == 0:
                            WD("vector", xfm_spill[0])
                        evacB[bidx] = emit(
                            "vector",
                            lambda e, pb=ptb[bidx % 2], cb=cb, tt=tt:
                            e.tensor_copy(xfm[:, cb * 4:(cb + 1) * 4, tt * 128:(tt + 1) * 128],
                                          pb[:, :].rearrange("p (c t) -> p c t", c=4)),
                            inc=True)
                        bidx += 1
                xfm_ready = cV.v
                W("gpsimd", sV, xfm_ready)
                xfm_spill[H] = dma("gpsimd", 4, xfm_sp[H], xfm[:, :, :])
                WD("tensor", c_wk)
                W("tensor", sV, xfm_ready)
                for tcol in range(2):
                    gidx = H * 2 + tcol
                    for m in range(8):
                        it = gidx * 8 + m
                        pidx = it % 2
                        if it >= 2:
                            pH, prem = divmod(it - 2, 16)
                            ptc, pm = divmod(prem, 8)
                            W("tensor", sV, k_evac[(pH, ptc, pm)])
                        for kc in range(32):
                            r = emit(
                                "tensor",
                                lambda e, p=pk[pidx], m=m, kc=kc, tcol=tcol:
                                e.matmul(p[:, :],
                                         wk_bf[:, kc, m * 128:(m + 1) * 128],
                                         xfm[:, kc, tcol * 512:(tcol + 1) * 512],
                                         start=(kc == 0), stop=(kc == 31)),
                                inc=(kc == 31))
                        kmm[(H, tcol, m)] = r
                        W("vector", sP, r)
                        if m == 0 and gidx >= 2:
                            WD("vector", k_spill[gidx - 2])
                        k_evac[(H, tcol, m)] = emit(
                            "vector",
                            lambda e, p=pk[pidx], m=m, tcol=tcol:
                            e.tensor_copy(kst[tcol % 2][:, m, :], p[:, :]), inc=True)
                    W("gpsimd", sV, k_evac[(H, tcol, 7)])
                    k_spill[gidx] = dma(
                        "gpsimd", 5 + gidx % 2,
                        k_sp[:, :, gidx * 512:(gidx + 1) * 512],
                        kst[tcol % 2][:, :, :])
            last_kmm = kmm[(1, 1, 7)]
            drain(); b.flush()

        with ExitStack() as esA:
            wv_bf = esA.enter_context(nc.sbuf_tensor("wv_bf", [128, 32, 1024], BF16))
            xfm = esA.enter_context(nc.sbuf_tensor("xfmV", [128, 32, 1024], BF16))
            vst = [esA.enter_context(nc.sbuf_tensor(f"vstA{i}", [128, 1024], BF16)) for i in range(2)]
            pv = [esA.enter_context(nc.psum_tensor(f"pvA{i}", [128, 1024], F32)) for i in range(2)]

            c_wv = dma("gpsimd", 1, wv_bf[:, :, :], wvt_d)
            v_evac = {}
            v_spill = {}
            vmm = {}
            for H in range(2):
                if H == 1:
                    W("sync", sP, vmm[7])
                c_xfm = dma("sync", 2, xfm[:, :, :], xfm_sp[H])
                WD("tensor", c_xfm)
                WD("tensor", c_wv)
                for ts in range(8):
                    kt = H * 8 + ts
                    if kt >= 2:
                        W("tensor", sV, v_evac[kt - 2])
                    for kc in range(32):
                        for nn in range(2):
                            r = emit(
                                "tensor",
                                lambda e, p=pv[kt % 2], kc=kc, nn=nn, ts=ts:
                                e.matmul(p[:, nn * 512:(nn + 1) * 512],
                                         xfm[:, kc, ts * 128:(ts + 1) * 128],
                                         wv_bf[:, kc, nn * 512:(nn + 1) * 512],
                                         start=(kc == 0), stop=(kc == 31)),
                                inc=(kc == 31 and nn == 1))
                    vmm[ts if H == 0 else ts + 8] = r
                    W("vector", sP, r)
                    if kt >= 2:
                        WD("vector", v_spill[kt - 2])
                    v_evac[kt] = emit("vector", lambda e, p=pv[kt % 2], kt=kt:
                                      e.tensor_copy(vst[kt % 2][:, :], p[:, :]), inc=True)
                    W("gpsimd", sV, v_evac[kt])
                    v_spill[kt] = dma("gpsimd", 3 + kt % 2, v_sp[:, kt, :], vst[kt % 2][:, :])
            drain(); b.flush()

        for H in range(2):
            with ExitStack() as esB:
                attn = esB.enter_context(
                    nc.sbuf_tensor(f"attnB{H}", [128, 32, 1024], BF16, side="right"))

                with ExitStack() as esB1:
                    xfm = esB1.enter_context(nc.sbuf_tensor(f"xfmB{H}", [128, 32, 1024], BF16))
                    kbuf = [esB1.enter_context(nc.sbuf_tensor(f"kbB{H}_{i}", [128, 2048], BF16)) for i in range(2)]
                    vbuf = [esB1.enter_context(nc.sbuf_tensor(f"vbB{H}_{i}", [128, 2048], BF16)) for i in range(2)]
                    wqb = [esB1.enter_context(nc.sbuf_tensor(f"wqB{H}_{i}", [128, DM], BF16)) for i in range(2)]
                    Qb = [esB1.enter_context(nc.sbuf_tensor(f"QbB{H}_{i}", [128, 512], BF16)) for i in range(2)]
                    Eb = [esB1.enter_context(nc.sbuf_tensor(f"EbB{H}_{i}", [128, 1024], BF16)) for i in range(3)]
                    rden = esB1.enter_context(nc.sbuf_tensor(f"rdenB{H}", [1, 512], F32))
                    rdbc = esB1.enter_context(nc.sbuf_tensor(f"rdbcB{H}", [128, 512], F32))
                    pq = [esB1.enter_context(nc.psum_tensor(f"pqB{H}_{i}", [128, 512], F32)) for i in range(2)]
                    ps = [esB1.enter_context(nc.psum_tensor(f"psB{H}_{i}", [128, 1024], F32)) for i in range(2)]
                    pd = esB1.enter_context(nc.psum_tensor(f"pdB{H}", [1, 512], F32))
                    ppv = esB1.enter_context(nc.psum_tensor(f"ppvB{H}", [128, 512], F32))

                    c_xfm = dma("sync", 2, xfm[:, :, :], xfm_sp[H])
                    kv_load = {}
                    wq_load = {}
                    q_done = {}; qev = {}; s_done = {}; pv_done = {}; d15 = {}
                    bc_done = {}; recip_done = {}; norm_done = {}; exp_done = {}

                    def q_mm_lams(g):
                        h, qc = divmod(g, 2)
                        lams = []
                        for kc in range(32):
                            lams.append(lambda e, kc=kc, h=h, qc=qc, g=g:
                                        e.matmul(pq[g % 2][:, :],
                                                 wqb[h % 2][:, kc * 128:(kc + 1) * 128],
                                                 xfm[:, kc, qc * 512:(qc + 1) * 512],
                                                 start=(kc == 0), stop=(kc == 31)))
                        return lams

                    wq_load[0] = dma("gpsimd", 5, wqb[0][:, :], wqt_d[0])
                    wq_load[1] = dma("gpsimd", 6, wqb[1][:, :], wqt_d[1])
                    WD("tensor", wq_load[0])
                    WD("tensor", c_xfm)
                    for i, lam in enumerate(q_mm_lams(0)):
                        q_done[0] = emit("tensor", lam, inc=(i == 31))
                    W("vector", sP, q_done[0])
                    qev[0] = emit("vector", lambda e: e.tensor_copy(Qb[0][:, :], pq[0][:, :]),
                                  inc=True)

                    for h in range(32):
                        hv = h // 4
                        if h % 4 == 0:
                            if hv >= 2:
                                lh = (hv - 2) * 4 + 3
                                W("sync", sP, pv_done[(lh * 2 + 1, 15)])
                            dma("sync", 3 + hv % 2, kbuf[hv % 2][:, :], k_sp[:, hv, :])
                            kv_load[hv] = dma("sync", 3 + hv % 2, vbuf[hv % 2][:, :],
                                              v_sp[:, :, hv * 128:(hv + 1) * 128])
                        for qc in range(2):
                            g = h * 2 + qc
                            if qc == 1 and h + 2 < 32:
                                W("gpsimd", sP, q_done[2 * h + 1])
                                wq_load[h + 2] = dma("gpsimd", 5 + h % 2, wqb[h % 2][:, :],
                                                     wqt_d[h + 2])
                            nxt = q_mm_lams(g + 1) if g + 1 < 64 else None
                            nxt_waits = []
                            if nxt is not None:
                                nh = (g + 1) // 2
                                if nh != h:
                                    nxt_waits.append(("D", wq_load[nh]))
                                if g >= 1:
                                    nxt_waits.append(("V", qev[g - 1]))
                            WD("tensor", kv_load[hv])
                            W("tensor", sV, qev[g])
                            if g >= 1:
                                W("tensor", sA, exp_done[(g - 1, 7)])
                                W("tensor", sV, recip_done[g - 1])
                                W("tensor", sV, norm_done[g - 1])
                                W("tensor", sV, bc_done[g - 1])

                            def mk_s(kt, g=g, hv=hv):
                                return lambda e: e.matmul(
                                    ps[(kt // 2) % 2][:, (kt % 2) * 512:(kt % 2 + 1) * 512],
                                    kbuf[hv % 2][:, kt * 128:(kt + 1) * 128],
                                    Qb[g % 2][:, :], start=True, stop=True)

                            def mk_d(kt, g=g):
                                return lambda e: e.matmul(
                                    pd[:, :], onec[:, :],
                                    Eb[(kt // 2) % 3][:, (kt % 2) * 512:(kt % 2 + 1) * 512],
                                    start=(kt == 0), stop=(kt == 15))

                            def mk_pv(kt, g=g, hv=hv):
                                return lambda e: e.matmul(
                                    ppv[:, :], vbuf[hv % 2][:, kt * 128:(kt + 1) * 128],
                                    Eb[(kt // 2) % 3][:, (kt % 2) * 512:(kt % 2 + 1) * 512],
                                    start=(kt == 0), stop=(kt == 15))

                            for kt in range(4):
                                s_done[(g, kt)] = emit("tensor", mk_s(kt), inc=(kt % 2 == 1))
                            for kt2 in range(8):
                                W("scalar", sP, s_done[(g, kt2 * 2 + 1)])
                                P = g * 8 + kt2 - 3
                                if P >= 0:
                                    gp, ktp = divmod(P, 8)
                                    W("scalar", sP, pv_done[(gp, ktp * 2 + 1)])
                                exp_done[(g, kt2)] = emit(
                                    "scalar",
                                    lambda e, kt2=kt2: e.activation(
                                        Eb[kt2 % 3][:, :], ps[kt2 % 2][:, :],
                                        AF.Exp, scale=SCALE), inc=True)
                                W("tensor", sA, exp_done[(g, kt2)])
                                for half in range(2):
                                    kt = kt2 * 2 + half
                                    r = emit("tensor", mk_d(kt), inc=(kt == 15))
                                    if kt == 15:
                                        d15[g] = r
                                    pv_done[(g, kt)] = emit("tensor", mk_pv(kt), inc=True)
                                if nxt is not None:
                                    for w in nxt_waits:
                                        if w[0] == "D":
                                            WD("tensor", w[1])
                                        else:
                                            W("tensor", sV, w[1])
                                    nxt_waits = []
                                    for i in range(4):
                                        lam = nxt[kt2 * 4 + i]
                                        q_done[g + 1] = emit("tensor", lam,
                                                             inc=(kt2 == 7 and i == 3))
                                if kt2 < 6:
                                    kt = kt2 * 2 + 4
                                    s_done[(g, kt)] = emit("tensor", mk_s(kt), inc=False)
                                    s_done[(g, kt + 1)] = emit("tensor", mk_s(kt + 1), inc=True)
                            W("vector", sP, d15[g])
                            recip_done[g] = emit("vector", lambda e:
                                                 e.reciprocal(rden[:, :], pd[:, :]), inc=True)
                            W("tensor", sV, recip_done[g])
                            bcmm = emit("tensor", lambda e:
                                        e.matmul(ps[0][:, 0:512], oner[:, :], rden[:, :],
                                                 start=True, stop=True), inc=True)
                            W("vector", sP, bcmm)
                            bc_done[g] = emit("vector", lambda e:
                                              e.tensor_copy(rdbc[:, :], ps[0][:, 0:512]),
                                              inc=True)
                            W("vector", sP, pv_done[(g, 15)])
                            norm_done[g] = emit(
                                "vector",
                                lambda e, h=h, qc=qc:
                                e.tensor_tensor(attn[:, h, qc * 512:(qc + 1) * 512],
                                                ppv[:, :], rdbc[:, :], ALU.mult), inc=True)
                            if nxt is not None:
                                W("vector", sP, q_done[g + 1])
                                qev[g + 1] = emit(
                                    "vector",
                                    lambda e, g=g: e.tensor_copy(Qb[(g + 1) % 2][:, :],
                                                                 pq[(g + 1) % 2][:, :]),
                                    inc=True)
                    drain(); b.flush()

                with ExitStack() as esB2:
                    atm = esB2.enter_context(nc.sbuf_tensor(f"atmB{H}", [128, 8, DM], BF16))
                    esB2i = esB2.enter_context(ExitStack())
                    wob = [esB2i.enter_context(nc.sbuf_tensor(f"woB{H}_{i}", [128, DM], BF16)) for i in range(2)]
                    woc = [esB2i.enter_context(nc.sbuf_tensor(f"wocB{H}_{i}", [128, 1024], BF16)) for i in range(2)]
                    pwo = [esB2i.enter_context(nc.psum_tensor(f"pwoB{H}_{i}", [128, 512], F32)) for i in range(2)]
                    ptb = [esB2i.enter_context(nc.psum_tensor(f"ptbB{H}_{i}", [128, 512], F32)) for i in range(2)]

                    woev = {}
                    tev = {}
                    tmm = {}
                    bi = 0
                    for oc in range(32):
                        if oc >= 2:
                            W("gpsimd", sP, tmm[(oc - 2, 1)])
                        c_wo = dma("gpsimd", 1 + oc % 2, wob[oc % 2][:, :], wot_d[oc])
                        WD("tensor", c_wo)
                        for tcol in range(2):
                            if bi >= 2:
                                W("tensor", sV, woev[bi - 2])
                            for dc in range(32):
                                r = emit(
                                    "tensor",
                                    lambda e, oc=oc, tcol=tcol, dc=dc:
                                    e.matmul(pwo[tcol % 2][:, :],
                                             wob[oc % 2][:, dc * 128:(dc + 1) * 128],
                                             attn[:, dc, tcol * 512:(tcol + 1) * 512],
                                             start=(dc == 0), stop=(dc == 31)),
                                    inc=(dc == 31))
                            W("vector", sP, r)
                            woev[bi] = emit(
                                "vector", lambda e, oc=oc, tcol=tcol:
                                e.tensor_copy(woc[oc % 2][:, tcol * 512:(tcol + 1) * 512],
                                              pwo[tcol % 2][:, :]), inc=True)
                            bi += 1
                        for tb in range(2):
                            gb = oc * 2 + tb
                            if gb >= 2:
                                W("tensor", sV, tev[gb - 2])
                            W("tensor", sV, woev[bi - 1])
                            for j in range(4):
                                ts = tb * 4 + j
                                r = emit(
                                    "tensor",
                                    lambda e, oc=oc, ts=ts, j=j, tb=tb:
                                    e.matmul(ptb[tb % 2][:, j * 128:(j + 1) * 128],
                                             woc[oc % 2][:, ts * 128:(ts + 1) * 128],
                                             ident[:, :], start=True, stop=True),
                                    inc=(j == 3))
                            tmm[(oc, tb)] = r
                            W("vector", sP, r)
                            tev[gb] = emit(
                                "vector",
                                lambda e, oc=oc, tb=tb:
                                e.tensor_copy(
                                    atm[:, tb * 4:(tb + 1) * 4, oc * 128:(oc + 1) * 128],
                                    ptb[tb % 2][:, :].rearrange("p (c t) -> p c t", c=4)),
                                inc=True)
                    drain(); b.flush()
                    esB2i.close()

                    with ExitStack() as esB3:
                        xres_ = esB3.enter_context(nc.sbuf_tensor(f"xrB{H}", [128, DM], F32))
                        xres = [xres_, xres_]
                        sres = [esB3.enter_context(nc.sbuf_tensor(f"srB{H}_{i}", [128, DM], F32)) for i in range(2)]
                        r1t = [esB3.enter_context(nc.sbuf_tensor(f"r1B{H}_{i}", [128, DM], BF16, side="right")) for i in range(2)]
                        st_ = esB3.enter_context(nc.sbuf_tensor(f"stB{H}", [128, 16], F32, side="right"))

                        addv = {}; nrm = {}; r1w = {}; sq_done = {}
                        for ts in range(8):
                            tg = H * 8 + ts
                            base = 8 * (ts % 2)
                            if ts >= 1:
                                W("sync", sV, addv[ts - 1])
                                W("sync", sA, sq_done[ts - 1])
                            c_x = dma("sync", 3, xres[ts % 2][:, :],
                                      x_d[tg * 128:(tg + 1) * 128, :])
                            WD("vector", c_x)
                            if ts >= 2:
                                W("vector", sV, nrm[ts - 2])
                            addv[ts] = emit(
                                "vector", lambda e, ts=ts:
                                e.tensor_tensor(sres[ts % 2][:, :], xres[ts % 2][:, :],
                                                atm[:, ts, :], ALU.add), inc=True)
                            W("scalar", sV, addv[ts])
                            sq_d = emit("scalar", lambda e, ts=ts, base=base:
                                        e.activation(xres[ts % 2][:, :], sres[ts % 2][:, :],
                                                     AF.Square,
                                                     accum_out=st_[:, base + 1:base + 2]),
                                        inc=True)
                            sq_done[ts] = sq_d
                            sum_d = emit("vector", lambda e, ts=ts, base=base:
                                         e.reduce_sum(st_[:, base:base + 1], sres[ts % 2][:, :],
                                                      axis=AXL.X), inc=True)
                            W("vector", sA, sq_d)
                            W("vector", sV, sum_d)
                            m1 = emit("vector", lambda e, base=base:
                                      e.tensor_scalar_mul(st_[:, base + 2:base + 3],
                                                          st_[:, base:base + 1], 1.0 / DM),
                                      inc=True)
                            m2 = emit("vector", lambda e, base=base:
                                      e.tensor_scalar_mul(st_[:, base + 3:base + 4],
                                                          st_[:, base + 1:base + 2], 1.0 / DM),
                                      inc=True)
                            W("vector", sV, m2)
                            m3 = emit("vector", lambda e, base=base:
                                      e.tensor_tensor(st_[:, base + 7:base + 8],
                                                      st_[:, base + 2:base + 3],
                                                      st_[:, base + 2:base + 3], ALU.mult),
                                      inc=True)
                            W("vector", sV, m3)
                            m4 = emit("vector", lambda e, base=base:
                                      e.scalar_tensor_tensor(st_[:, base + 4:base + 5],
                                                             st_[:, base + 3:base + 4], EPS,
                                                             st_[:, base + 7:base + 8],
                                                             ALU.add, ALU.subtract), inc=True)
                            W("scalar", sV, m4)
                            sqr = emit("scalar", lambda e, base=base:
                                       e.activation(st_[:, base + 5:base + 6],
                                                    st_[:, base + 4:base + 5], AF.Sqrt),
                                       inc=True)
                            W("vector", sA, sqr)
                            rcp = emit("vector", lambda e, base=base:
                                       e.reciprocal(st_[:, base + 6:base + 7],
                                                    st_[:, base + 5:base + 6]), inc=True)
                            W("vector", sV, rcp)
                            if ts >= 2:
                                WD("vector", r1w[ts - 2])
                            nrm[ts] = emit(
                                "vector", lambda e, ts=ts, base=base:
                                e.tensor_scalar(r1t[ts % 2][:, :], sres[ts % 2][:, :],
                                                st_[:, base + 2:base + 3],
                                                st_[:, base + 6:base + 7],
                                                ALU.subtract, ALU.mult), inc=True)
                            W("gpsimd", sV, nrm[ts])
                            r1w[ts] = dma("gpsimd", 5 + ts % 2, r1_sp[tg * 128:(tg + 1) * 128, :],
                                          r1t[ts % 2][:, :])
                        drain(); b.flush()

        if dump:
            c = dma("gpsimd", 0, dbg["r1"], r1_sp[:, :])
            drain()
            b.flush()
        if dump == 1:
            drain()
            b.flush()
            return nc

        with ExitStack() as esC:
            r1fm = esC.enter_context(nc.sbuf_tensor("r1fm", [128, 32, 2048], BF16, side="right"))

            with ExitStack() as esC0:
                r1tm = [esC0.enter_context(nc.sbuf_tensor(f"r1tmC{i}", [128, DM], BF16)) for i in range(2)]
                ptb = [esC0.enter_context(nc.psum_tensor(f"ptbC{i}", [128, 512], F32)) for i in range(2)]
                tmmc = {}; tevc = {}
                bi = 0
                for ts in range(16):
                    if ts >= 2:
                        W("sync", sP, tmmc[ts - 2])
                    c_r = dma("sync", 1 + ts % 2, r1tm[ts % 2][:, :], r1_sp[ts * 128:(ts + 1) * 128, :])
                    WD("tensor", c_r)
                    for cb in range(8):
                        if bi >= 2:
                            W("tensor", sV, tevc[bi - 2])
                        for j in range(4):
                            c = cb * 4 + j
                            r = emit(
                                "tensor",
                                lambda e, pb=ptb[bi % 2], ts=ts, c=c, j=j:
                                e.matmul(pb[:, j * 128:(j + 1) * 128],
                                         r1tm[ts % 2][:, c * 128:(c + 1) * 128],
                                         ident[:, :], start=True, stop=True),
                                inc=(j == 3))
                        tmmc[ts] = r
                        W("vector", sP, r)
                        tevc[bi] = emit(
                            "vector",
                            lambda e, pb=ptb[bi % 2], cb=cb, ts=ts:
                            e.tensor_copy(r1fm[:, cb * 4:(cb + 1) * 4, ts * 128:(ts + 1) * 128],
                                          pb[:, :].rearrange("p (c t) -> p c t", c=4)),
                            inc=True)
                        bi += 1
                drain(); b.flush()

            with ExitStack() as esC1:
                qq = {m: [esC1.enter_context(nc.sbuf_tensor(f"qq{m}{i}", [128, 1024], I32))
                          for i in range(2)] for m in range(2)}
                szt = {m: [esC1.enter_context(nc.sbuf_tensor(f"szt{m}{i}", [128, 64], F32))
                           for i in range(2)] for m in range(2)}
                w1 = {m: esC1.enter_context(nc.sbuf_tensor(f"w1C{m}", [128, DM], BF16))
                      for m in range(2)}
                wT = {m: esC1.enter_context(nc.sbuf_tensor(f"wTC{m}", [128, DM], BF16))
                      for m in range(2)}
                sgb = [esC1.enter_context(nc.sbuf_tensor(f"sgbC{i}", [128, 512], BF16)) for i in range(4)]
                hst = [esC1.enter_context(nc.sbuf_tensor(f"hstC{i}", [128, 512], BF16)) for i in range(4)]
                ptb = [esC1.enter_context(nc.psum_tensor(f"ptbC1{i}", [128, 512], F32)) for i in range(2)]
                pg = [esC1.enter_context(nc.psum_tensor(f"pgC{i}", [128, 512], F32)) for i in range(2)]
                pu = [esC1.enter_context(nc.psum_tensor(f"puC{i}", [128, 512], F32)) for i in range(4)]

                qsrc = {0: qg_d, 1: qu_d}
                ssrc = {0: sg_d, 1: su_d}
                zsrc = {0: zg_d, 1: zu_d}
                qload = {}; szload = {}; deq = {}; mmt = {}; tevc = {}
                mmg = {}; silu_d = {}; hmul = {}; hw = {}
                HWCH = {0: 7, 1: 8, 2: 9, 3: 0}

                def emit_hmul(ic):
                    rows_ = slice(ic * 128, (ic + 1) * 128)
                    for tcol in range(4):
                        W("vector", sP, mmg[(ic, 1, tcol)])
                        W("vector", sA, silu_d[(ic, tcol)])
                        if (ic - 1, tcol) in hw:
                            WD("vector", hw[(ic - 1, tcol)])
                        hmul[(ic, tcol)] = emit(
                            "vector",
                            lambda e, tcol=tcol:
                            e.tensor_tensor(hst[tcol][:, :], pu[tcol][:, :],
                                            sgb[tcol][:, :], ALU.mult), inc=True)
                        W("gpsimd", sV, hmul[(ic, tcol)])
                        hw[(ic, tcol)] = dma("gpsimd", HWCH[tcol],
                                             h_sp[rows_, tcol * 512:(tcol + 1) * 512],
                                             hst[tcol][:, :])

                bti = 0
                for ic in range(NIC):
                    rows = slice(ic * 128, (ic + 1) * 128)
                    for m in range(2):
                        if ic >= 2:
                            W("sync", sV, deq[(ic - 2, m, 3)])
                        dma("sync", 5 + m, szt[m][ic % 2][:, 0:32], ssrc[m][rows, :])
                        szload[(ic, m)] = dma("sync", 5 + m, szt[m][ic % 2][:, 32:64],
                                              zsrc[m][rows, :])
                        for j in range(2):
                            if ic >= 1:
                                W("sync", sV, deq[(ic - 1, m, j + 2)])
                            qload[(ic, m, j)] = dma(
                                "sync", 1 + m * 2 + j % 2, qq[m][j % 2][:, :],
                                qsrc[m][rows, j * 1024:(j + 1) * 1024])
                    for m in range(2):
                        if ic >= 1:
                            W("vector", sP, mmt[(ic - 1, m)])
                        WD("vector", szload[(ic, m)])
                        for j in range(4):
                            WD("vector", qload[(ic, m, j)])
                            for g8 in range(8):
                                g = j * 8 + g8
                                deq[(ic, m, j)] = emit(
                                    "vector",
                                    lambda e, m=m, j=j, g=g, g8=g8, ic=ic:
                                    e.tensor_scalar(w1[m][:, g * 128:(g + 1) * 128],
                                                    qq[m][j % 2][:, g8 * 128:(g8 + 1) * 128],
                                                    szt[m][ic % 2][:, 32 + g:33 + g],
                                                    szt[m][ic % 2][:, g:g + 1],
                                                    ALU.subtract, ALU.mult),
                                    inc=(g8 == 7))
                            if j + 2 <= 3:
                                W("sync", sV, deq[(ic, m, j)])
                                qload[(ic, m, j + 2)] = dma(
                                    "sync", 1 + m * 2 + j % 2, qq[m][j % 2][:, :],
                                    qsrc[m][rows, (j + 2) * 1024:(j + 3) * 1024])
                    for m in range(2):
                        if ic >= 1:
                            W("vector", sP, mmg[(ic - 1, m, 3)])
                        for cb in range(8):
                            if bti >= 2:
                                W("tensor", sV, tevc[bti - 2])
                            W("tensor", sV, deq[(ic, m, cb // 2)])
                            for j in range(4):
                                g = cb * 4 + j
                                r = emit(
                                    "tensor",
                                    lambda e, pb=ptb[bti % 2], m=m, g=g, j=j:
                                    e.matmul(pb[:, j * 128:(j + 1) * 128],
                                             w1[m][:, g * 128:(g + 1) * 128],
                                             ident[:, :], start=True, stop=True),
                                    inc=(j == 3))
                            W("vector", sP, r)
                            tevc[bti] = emit(
                                "vector",
                                lambda e, pb=ptb[bti % 2], m=m, cb=cb:
                                e.tensor_copy(wT[m][:, cb * 512:(cb + 1) * 512], pb[:, :]),
                                inc=True)
                            bti += 1
                        mmt[(ic, m)] = r
                    wT_ready = tevc[bti - 1]
                    if ic >= 1:
                        emit_hmul(ic - 1)
                    for tcol in range(4):
                        for m in range(2):
                            pp = pg[tcol % 2] if m == 0 else pu[tcol]
                            W("tensor", sV, wT_ready)
                            if m == 0:
                                prev = silu_d.get((ic, tcol - 2), silu_d.get((ic - 1, tcol + 2)))
                                if prev:
                                    W("tensor", sA, prev)
                            else:
                                if (ic - 1, tcol) in hmul:
                                    W("tensor", sV, hmul[(ic - 1, tcol)])
                            for kc in range(32):
                                r = emit(
                                    "tensor",
                                    lambda e, pp=pp, m=m, kc=kc, tcol=tcol:
                                    e.matmul(pp[:, :],
                                             wT[m][:, kc * 128:(kc + 1) * 128],
                                             r1fm[:, kc, tcol * 512:(tcol + 1) * 512],
                                             start=(kc == 0), stop=(kc == 31)),
                                    inc=(kc == 31))
                            mmg[(ic, m, tcol)] = r
                        W("scalar", sP, mmg[(ic, 0, tcol)])
                        if (ic - 1, tcol) in hmul:
                            W("scalar", sV, hmul[(ic - 1, tcol)])
                        silu_d[(ic, tcol)] = emit(
                            "scalar",
                            lambda e, tcol=tcol: e.activation(sgb[tcol][:, :],
                                                              pg[tcol % 2][:, :], AF.Silu),
                            inc=True)
                emit_hmul(NIC - 1)
                drain(); b.flush()

        with ExitStack() as esC2:
            sdt = esC2.enter_context(nc.sbuf_tensor("sdtC", [128, 32, NIC], F32))
            zdt = esC2.enter_context(nc.sbuf_tensor("zdtC", [128, 32, NIC], F32))
            qsl = [esC2.enter_context(nc.sbuf_tensor(f"qslC{i}", [128, DM], I32)) for i in range(2)]
            w1d = [esC2.enter_context(nc.sbuf_tensor(f"w1dC{i}", [128, DM], BF16)) for i in range(2)]
            stg = [esC2.enter_context(nc.sbuf_tensor(f"stgC{i}", [128, DM], BF16)) for i in range(2)]
            ptb = [esC2.enter_context(nc.psum_tensor(f"ptbC2{i}", [128, 512], F32)) for i in range(2)]

            dma("sync", 5, sdt[:, :, :], sdt_d)
            c_sz = dma("sync", 5, zdt[:, :, :], zdt_d)
            WD("vector", c_sz)
            ql = {}; dq = {}; mt = {}; tev2 = {}; stw = {}
            bti = 0
            for gi in range(NIC):
                if gi >= 2:
                    W("sync", sV, dq[gi - 2])
                ql[gi] = dma("sync", 1 + gi % 2, qsl[gi % 2][:, :], qdt_d[gi])
                WD("vector", ql[gi])
                if gi >= 2:
                    W("vector", sP, mt[gi - 2])
                for hoc in range(32):
                    dq[gi] = emit(
                        "vector",
                        lambda e, gi=gi, hoc=hoc:
                        e.tensor_scalar(w1d[gi % 2][:, hoc * 128:(hoc + 1) * 128],
                                        qsl[gi % 2][:, hoc * 128:(hoc + 1) * 128],
                                        zdt[:, hoc, gi:gi + 1], sdt[:, hoc, gi:gi + 1],
                                        ALU.subtract, ALU.mult),
                        inc=(hoc == 31))
                W("tensor", sV, dq[gi])
                for cb in range(8):
                    if bti >= 2:
                        W("tensor", sA, tev2[bti - 2])
                    for j in range(4):
                        hoc = cb * 4 + j
                        r = emit(
                            "tensor",
                            lambda e, pb=ptb[bti % 2], gi=gi, hoc=hoc, j=j:
                            e.matmul(pb[:, j * 128:(j + 1) * 128],
                                     w1d[gi % 2][:, hoc * 128:(hoc + 1) * 128],
                                     ident[:, :], start=True, stop=True),
                            inc=(j == 3))
                    mt[gi] = r
                    W("scalar", sP, r)
                    if cb == 0 and gi >= 2:
                        WD("scalar", stw[gi - 2])
                    tev2[bti] = emit(
                        "scalar",
                        lambda e, pb=ptb[bti % 2], gi=gi, cb=cb:
                        e.copy(stg[gi % 2][:, cb * 512:(cb + 1) * 512], pb[:, :]),
                        inc=True)
                    bti += 1
                W("gpsimd", sA, tev2[bti - 1])
                stw[gi] = dma("gpsimd", 3 + gi % 2, wdt_sp[gi * 128:(gi + 1) * 128, :], stg[gi % 2][:, :])
            drain(); b.flush()

        if dump == 2:
            dma("gpsimd", 0, dbg["h"], h_sp[:, 0:512])
            dma("gpsimd", 0, dbg["wdt"], wdt_sp[0:512, :])
            drain()
            b.flush()
            return nc

        with ExitStack() as esD:
            ht = esD.enter_context(nc.sbuf_tensor("htD", [128, NIC, 512], BF16))
            ftm = esD.enter_context(nc.sbuf_tensor("ftmD", [128, 4, DM], F32))
            wdb = [esD.enter_context(nc.sbuf_tensor(f"wdbD{i}", [128, 1024], BF16)) for i in range(4)]
            r1q = esD.enter_context(nc.sbuf_tensor("r1qD", [128, 4, 1024], BF16))
            obuf = esD.enter_context(nc.sbuf_tensor("obufD", [128, DM], F32))
            st_ = esD.enter_context(nc.sbuf_tensor("stD", [128, 16], F32))
            pf = [esD.enter_context(nc.psum_tensor(f"pfD{i}", [128, 1024], F32)) for i in range(4)]

            WDCH = {0: 2, 1: 3, 2: 6, 3: 7}
            hload = {}; dmm = {}; fev = {}; r1l = {}; outw = {}
            for tm in range(4):
                tc = slice(tm * 512, (tm + 1) * 512)
                for ic in range(NIC):
                    if tm >= 1:
                        W("sync", sP, dmm[(tm - 1, 3, ic)])
                    hload[(tm, ic)] = dma("sync", 1, ht[:, ic, :],
                                          h_sp[ic * 128:(ic + 1) * 128, tc])
                for hoq in range(4):
                    hc = slice(hoq * 1024, (hoq + 1) * 1024)
                    for ts in range(4):
                        prevf = fev.get((tm, hoq - 1, ts), fev.get((tm - 1, 3, ts)))
                        if prevf:
                            W("sync", sV, prevf)
                        r1l[(tm, hoq, ts)] = dma(
                            "sync", 4 + ts // 2, r1q[:, ts, :],
                            r1_sp[tm * 512 + ts * 128: tm * 512 + (ts + 1) * 128, hc])
                    for ic in range(NIC):
                        par = ic % 4
                        gmm = dmm.get((tm, hoq, ic - 4), dmm.get((tm, hoq - 1, NIC - 4 + ic),
                                                                 dmm.get((tm - 1, 3, NIC - 4 + ic))))
                        if gmm:
                            W("sync", sP, gmm)
                        c_w = dma("sync", WDCH[par], wdb[par][:, :],
                                  wdt_sp[ic * 128:(ic + 1) * 128, hc])
                        WD("tensor", c_w)
                        if hoq == 0:
                            WD("tensor", hload[(tm, ic)])
                        if ic == 0:
                            for ts in range(4):
                                prevf = fev.get((tm, hoq - 1, ts), fev.get((tm - 1, 3, ts)))
                                if prevf:
                                    W("tensor", sV, prevf)
                        for ts in range(4):
                            for nn in range(2):
                                r = emit(
                                    "tensor",
                                    lambda e, par=par, ts=ts, nn=nn, ic=ic:
                                    e.matmul(pf[ts][:, nn * 512:(nn + 1) * 512],
                                             ht[:, ic, ts * 128:(ts + 1) * 128],
                                             wdb[par][:, nn * 512:(nn + 1) * 512],
                                             start=(ic == 0), stop=(ic == NIC - 1)),
                                    inc=(ts == 3 and nn == 1))
                        dmm[(tm, hoq, ic)] = r
                    for ts in range(4):
                        W("vector", sP, dmm[(tm, hoq, NIC - 1)])
                        WD("vector", r1l[(tm, hoq, (ts // 2) * 2 + 1)])
                        fev[(tm, hoq, ts)] = emit(
                            "vector",
                            lambda e, ts=ts, hoq=hoq:
                            e.tensor_tensor(ftm[:, ts, hoq * 1024:(hoq + 1) * 1024],
                                            pf[ts][:, :], r1q[:, ts, :], ALU.add), inc=True)
                for ts in range(4):
                    tg = tm * 4 + ts
                    base = 8 * (ts % 2)
                    W("scalar", sV, fev[(tm, 3, ts)])
                    if tg >= 1:
                        WD("scalar", outw[tg - 1])
                    sq_d = emit("scalar", lambda e, ts=ts, base=base:
                                e.activation(obuf[:, :], ftm[:, ts, :], AF.Square,
                                             accum_out=st_[:, base + 1:base + 2]), inc=True)
                    W("vector", sV, fev[(tm, 3, ts)])
                    sum_d = emit("vector", lambda e, ts=ts, base=base:
                                 e.reduce_sum(st_[:, base:base + 1], ftm[:, ts, :],
                                              axis=AXL.X), inc=True)
                    W("vector", sA, sq_d)
                    W("vector", sV, sum_d)
                    m1 = emit("vector", lambda e, base=base:
                              e.tensor_scalar_mul(st_[:, base + 2:base + 3],
                                                  st_[:, base:base + 1], 1.0 / DM), inc=True)
                    m2 = emit("vector", lambda e, base=base:
                              e.tensor_scalar_mul(st_[:, base + 3:base + 4],
                                                  st_[:, base + 1:base + 2], 1.0 / DM), inc=True)
                    W("vector", sV, m2)
                    m3 = emit("vector", lambda e, base=base:
                              e.tensor_tensor(st_[:, base + 7:base + 8],
                                              st_[:, base + 2:base + 3],
                                              st_[:, base + 2:base + 3], ALU.mult), inc=True)
                    W("vector", sV, m3)
                    m4 = emit("vector", lambda e, base=base:
                              e.scalar_tensor_tensor(st_[:, base + 4:base + 5],
                                                     st_[:, base + 3:base + 4], EPS,
                                                     st_[:, base + 7:base + 8],
                                                     ALU.add, ALU.subtract), inc=True)
                    W("scalar", sV, m4)
                    sqr = emit("scalar", lambda e, base=base:
                               e.activation(st_[:, base + 5:base + 6],
                                            st_[:, base + 4:base + 5], AF.Sqrt), inc=True)
                    W("vector", sA, sqr)
                    rcp = emit("vector", lambda e, base=base:
                               e.reciprocal(st_[:, base + 6:base + 7],
                                            st_[:, base + 5:base + 6]), inc=True)
                    W("vector", sV, rcp)
                    W("vector", sA, sq_d)
                    if tg >= 1:
                        WD("vector", outw[tg - 1])
                    nrm = emit("vector", lambda e, ts=ts, base=base:
                               e.tensor_scalar(obuf[:, :], ftm[:, ts, :],
                                               st_[:, base + 2:base + 3],
                                               st_[:, base + 6:base + 7],
                                               ALU.subtract, ALU.mult), inc=True)
                    W("gpsimd", sV, nrm)
                    outw[tg] = dma("gpsimd", 8, out_d[tg * 128:(tg + 1) * 128, :], obuf[:, :])
            drain()
            b.flush()

    return nc


def _host_prep(inputs):
    Wq = np.asarray(inputs["L0_attn_Wq"])
    Wk = np.asarray(inputs["L0_attn_Wk"])
    Wv = np.asarray(inputs["L0_attn_Wv"])
    Wo = np.asarray(inputs["L0_attn_Wo"])
    qd = np.asarray(inputs["L0_ffn_qw_down"])
    sd = np.asarray(inputs["L0_ffn_s_down"])
    zd = np.asarray(inputs["L0_ffn_z_down"])
    common = {
        "wqt_d": np.ascontiguousarray(Wq.T.reshape(32, 128, 32, 128).transpose(2, 1, 0, 3)),
        "wkt_d": np.ascontiguousarray(Wk.T.reshape(32, 128, 1024).transpose(1, 0, 2)),
        "wvt_d": np.ascontiguousarray(Wv.T.reshape(32, 128, 1024).transpose(1, 0, 2)),
        "wot_d": np.ascontiguousarray(Wo.T.reshape(32, 128, 32, 128).transpose(2, 1, 0, 3)),
        "qg_d": np.ascontiguousarray(inputs["L0_ffn_qw_gate"]),
        "qu_d": np.ascontiguousarray(inputs["L0_ffn_qw_up"]),
        "sg_d": np.ascontiguousarray(inputs["L0_ffn_s_gate"]),
        "zg_d": np.ascontiguousarray(inputs["L0_ffn_z_gate"]),
        "su_d": np.ascontiguousarray(inputs["L0_ffn_s_up"]),
        "zu_d": np.ascontiguousarray(inputs["L0_ffn_z_up"]),
        "qdt_d": np.ascontiguousarray(qd.reshape(32, 128, 86, 128).transpose(2, 1, 0, 3)),
        "sdt_d": np.ascontiguousarray(sd.reshape(32, 128, 86).transpose(1, 0, 2)),
        "zdt_d": np.ascontiguousarray(zd.reshape(32, 128, 86).transpose(1, 0, 2)),
        "ident_d": np.eye(128, dtype=bfloat16),
        "onec_d": np.ones((128, 1), dtype=bfloat16),
        "oner_d": np.ones((1, 128), dtype=np.float32),
    }
    x = np.asarray(inputs["L0_attn_x"])
    maps = []
    for core in range(8):
        m = dict(common)
        m["x_d"] = np.ascontiguousarray(x[core])
        maps.append(m)
    return maps


def kernel(**inputs):
    maps = _host_prep(inputs)
    nc = build_nc(dump=0)
    res = run_bass_kernel_spmd(nc, maps, core_ids=list(range(8)))
    out = np.stack([np.asarray(res.results[c]["out_d"]) for c in range(8)], axis=0)
    return out.astype(np.float32)


# version 09
# speedup vs baseline: 1.0009x; 1.0009x over previous
import math
from contextlib import ExitStack

import numpy as np
import ml_dtypes

import concourse.bass as bass
import concourse.mybir as mybir
from concourse.bass_utils import run_bass_kernel_spmd

BF16 = mybir.dt.bfloat16
F32 = mybir.dt.float32
I32 = mybir.dt.int32
AF = mybir.ActivationFunctionType
ALU = mybir.AluOpType
AXL = mybir.AxisListType
bfloat16 = ml_dtypes.bfloat16

T = 2048
DM = 4096
NIC = 86
II = 11008
SCALE = 1.0 / math.sqrt(128.0)
EPS = 1e-5
ENGINES = ("sync", "gpsimd", "tensor", "scalar", "vector")


class Ctr:
    __slots__ = ("v",)

    def __init__(self):
        self.v = 0


class Builder:
    def __init__(self, nc):
        self.nc = nc
        self.streams = {k: [] for k in ENGINES}

    def op(self, eng, fn):
        self.streams[eng].append(fn)

    def wait(self, eng, sem, val):
        if val:
            self.streams[eng].append(lambda e, s=sem, v=val: e.wait_ge(s, v))

    def flush(self):
        nc = self.nc
        streams = self.streams
        with nc.Block() as blk:
            for name in ENGINES:
                ops = streams[name]

                def runner(e, ops=ops):
                    for f in ops:
                        f(e)

                getattr(blk, name)(runner)
        self.streams = {k: [] for k in ENGINES}


def build_nc(dump=0):
    nc = bass.Bass("TRN2", target_bir_lowering=False)

    x_d = nc.dram_tensor("x_d", [T, DM], F32, kind="ExternalInput").ap()
    wqt_d = nc.dram_tensor("wqt_d", [32, 128, 32, 128], F32, kind="ExternalInput").ap()
    wkt_d = nc.dram_tensor("wkt_d", [128, 32, 1024], F32, kind="ExternalInput").ap()
    wvt_d = nc.dram_tensor("wvt_d", [128, 32, 1024], F32, kind="ExternalInput").ap()
    wot_d = nc.dram_tensor("wot_d", [32, 128, 32, 128], F32, kind="ExternalInput").ap()
    qg_d = nc.dram_tensor("qg_d", [II, DM], I32, kind="ExternalInput").ap()
    qu_d = nc.dram_tensor("qu_d", [II, DM], I32, kind="ExternalInput").ap()
    sg_d = nc.dram_tensor("sg_d", [II, 32], F32, kind="ExternalInput").ap()
    zg_d = nc.dram_tensor("zg_d", [II, 32], F32, kind="ExternalInput").ap()
    su_d = nc.dram_tensor("su_d", [II, 32], F32, kind="ExternalInput").ap()
    zu_d = nc.dram_tensor("zu_d", [II, 32], F32, kind="ExternalInput").ap()
    qdt_d = nc.dram_tensor("qdt_d", [NIC, 128, 32, 128], I32, kind="ExternalInput").ap()
    sdt_d = nc.dram_tensor("sdt_d", [128, 32, NIC], F32, kind="ExternalInput").ap()
    zdt_d = nc.dram_tensor("zdt_d", [128, 32, NIC], F32, kind="ExternalInput").ap()
    ident_d = nc.dram_tensor("ident_d", [128, 128], BF16, kind="ExternalInput").ap()
    onec_d = nc.dram_tensor("onec_d", [128, 1], BF16, kind="ExternalInput").ap()
    oner_d = nc.dram_tensor("oner_d", [1, 128], F32, kind="ExternalInput").ap()
    out_d = nc.dram_tensor("out_d", [T, DM], F32, kind="ExternalOutput").ap()

    xfm_sp = nc.dram_tensor("xfm_sp", [2, 128, 32, 1024], BF16).ap()
    k_sp = nc.dram_tensor("k_sp", [128, 8, 2048], BF16).ap()
    v_sp = nc.dram_tensor("v_sp", [128, 16, 1024], BF16).ap()
    r1_sp = nc.dram_tensor("r1_sp", [T, DM], BF16).ap()
    h_sp = nc.dram_tensor("h_sp", [II, T], BF16).ap()
    wdt_sp = nc.dram_tensor("wdt_sp", [II, DM], BF16).ap()
    dbg = {}
    if dump:
        dbg["r1"] = nc.dram_tensor("dbg_r1", [T, DM], F32, kind="ExternalOutput").ap()
    if dump >= 2:
        dbg["h"] = nc.dram_tensor("dbg_h", [II, 512], F32, kind="ExternalOutput").ap()
        dbg["wdt"] = nc.dram_tensor("dbg_wdt", [512, DM], F32, kind="ExternalOutput").ap()

    es = ExitStack()
    with es:
        NDMA = 10
        dsem = [es.enter_context(nc.semaphore(f"d{i}")) for i in range(NDMA)]
        dctr = [Ctr() for _ in range(NDMA)]
        sP = es.enter_context(nc.semaphore("sP"))
        sA = es.enter_context(nc.semaphore("sA"))
        sV = es.enter_context(nc.semaphore("sV"))
        cP, cA, cV = Ctr(), Ctr(), Ctr()
        SEM = {"tensor": (sP, cP), "scalar": (sA, cA), "vector": (sV, cV)}

        ident = es.enter_context(nc.sbuf_tensor("ident", [128, 128], BF16))
        onec = es.enter_context(nc.sbuf_tensor("onec", [128, 1], BF16))
        oner = es.enter_context(nc.sbuf_tensor("oner", [1, 128], F32))

        b = Builder(nc)
        W = b.wait

        def dma(eng, ch, out, in_):
            s, c = dsem[ch], dctr[ch]
            c.v += 16
            b.op(eng, lambda e, o=out, i=in_, s=s: e.dma_start(out=o, in_=i).then_inc(s, 16))
            return (s, c.v)

        def WD(eng, tok):
            if tok:
                b.wait(eng, tok[0], tok[1])

        def emit(eng, fn, inc=False):
            s, c = SEM[eng]
            if inc:
                c.v += 1
                b.op(eng, lambda e, f=fn, s=s: f(e).then_inc(s, 1))
                return c.v
            b.op(eng, fn)
            return None

        def drain():
            for i in range(NDMA):
                if dctr[i].v:
                    W("gpsimd", dsem[i], dctr[i].v)
                    W("sync", dsem[i], dctr[i].v)

        dma("sync", 0, ident[:, :], ident_d)
        dma("sync", 0, onec[:, :], onec_d)
        c_consts = dma("sync", 0, oner[:, :], oner_d)
        for eng in ("tensor", "vector", "scalar", "gpsimd"):
            WD(eng, c_consts)
        drain(); b.flush()

        with ExitStack() as esA:
            wk_bf = esA.enter_context(nc.sbuf_tensor("wk_bf", [128, 32, 1024], BF16))
            xfm = esA.enter_context(nc.sbuf_tensor("xfmA", [128, 32, 1024], BF16))
            xb = [esA.enter_context(nc.sbuf_tensor(f"xbA{i}", [128, DM], BF16)) for i in range(2)]
            kst = [esA.enter_context(nc.sbuf_tensor(f"kstA{i}", [128, 8, 512], BF16)) for i in range(2)]
            ptb = [esA.enter_context(nc.psum_tensor(f"ptbA{i}", [128, 512], F32)) for i in range(2)]
            pk = [esA.enter_context(nc.psum_tensor(f"pkA{i}", [128, 512], F32)) for i in range(2)]

            c_wk = None

            evacB = {}
            mmT_last = {}
            k_evac = {}
            k_spill = {}
            kmm = {}
            xfm_spill = {}
            bidx = 0
            for H in range(2):
                for tt in range(8):
                    tg = H * 8 + tt
                    if tg >= 2:
                        W("gpsimd", sP, mmT_last[tg - 2])
                    c_x = dma("gpsimd", 2 + tg % 2, xb[tg % 2][:, :], x_d[tg * 128:(tg + 1) * 128, :])
                    if tg == 1:
                        c_wk = dma("gpsimd", 1, wk_bf[:, :, :], wkt_d)
                    WD("tensor", c_x)
                    for cb in range(8):
                        if bidx >= 2:
                            W("tensor", sV, evacB[bidx - 2])
                        for j in range(4):
                            c = cb * 4 + j
                            r = emit("tensor",
                                     lambda e, pb=ptb[bidx % 2], xt=xb[tg % 2], c=c, j=j:
                                     e.matmul(pb[:, j * 128:(j + 1) * 128],
                                              xt[:, c * 128:(c + 1) * 128], ident[:, :],
                                              start=True, stop=True), inc=(j == 3))
                        mmT_last[tg] = r
                        W("vector", sP, r)
                        if tg == 8 and cb == 0:
                            WD("vector", xfm_spill[0])
                        evacB[bidx] = emit(
                            "vector",
                            lambda e, pb=ptb[bidx % 2], cb=cb, tt=tt:
                            e.tensor_copy(xfm[:, cb * 4:(cb + 1) * 4, tt * 128:(tt + 1) * 128],
                                          pb[:, :].rearrange("p (c t) -> p c t", c=4)),
                            inc=True)
                        bidx += 1
                xfm_ready = cV.v
                W("gpsimd", sV, xfm_ready)
                xfm_spill[H] = dma("gpsimd", 4, xfm_sp[H], xfm[:, :, :])
                WD("tensor", c_wk)
                W("tensor", sV, xfm_ready)
                for tcol in range(2):
                    gidx = H * 2 + tcol
                    for m in range(8):
                        it = gidx * 8 + m
                        pidx = it % 2
                        if it >= 2:
                            pH, prem = divmod(it - 2, 16)
                            ptc, pm = divmod(prem, 8)
                            W("tensor", sV, k_evac[(pH, ptc, pm)])
                        for kc in range(32):
                            r = emit(
                                "tensor",
                                lambda e, p=pk[pidx], m=m, kc=kc, tcol=tcol:
                                e.matmul(p[:, :],
                                         wk_bf[:, kc, m * 128:(m + 1) * 128],
                                         xfm[:, kc, tcol * 512:(tcol + 1) * 512],
                                         start=(kc == 0), stop=(kc == 31)),
                                inc=(kc == 31))
                        kmm[(H, tcol, m)] = r
                        W("vector", sP, r)
                        if m == 0 and gidx >= 2:
                            WD("vector", k_spill[gidx - 2])
                        k_evac[(H, tcol, m)] = emit(
                            "vector",
                            lambda e, p=pk[pidx], m=m, tcol=tcol:
                            e.tensor_copy(kst[tcol % 2][:, m, :], p[:, :]), inc=True)
                    W("gpsimd", sV, k_evac[(H, tcol, 7)])
                    k_spill[gidx] = dma(
                        "gpsimd", 5 + gidx % 2,
                        k_sp[:, :, gidx * 512:(gidx + 1) * 512],
                        kst[tcol % 2][:, :, :])
            last_kmm = kmm[(1, 1, 7)]
            drain(); b.flush()

        with ExitStack() as esA:
            wv_bf = esA.enter_context(nc.sbuf_tensor("wv_bf", [128, 32, 1024], BF16))
            xfm = esA.enter_context(nc.sbuf_tensor("xfmV", [128, 32, 1024], BF16))
            vst = [esA.enter_context(nc.sbuf_tensor(f"vstA{i}", [128, 1024], BF16)) for i in range(2)]
            pv = [esA.enter_context(nc.psum_tensor(f"pvA{i}", [128, 1024], F32)) for i in range(2)]

            c_wv = dma("gpsimd", 1, wv_bf[:, :, :], wvt_d)
            v_evac = {}
            v_spill = {}
            vmm = {}
            for H in range(2):
                if H == 1:
                    W("sync", sP, vmm[7])
                c_xfm = dma("sync", 2, xfm[:, :, :], xfm_sp[H])
                WD("tensor", c_xfm)
                WD("tensor", c_wv)
                for ts in range(8):
                    kt = H * 8 + ts
                    if kt >= 2:
                        W("tensor", sV, v_evac[kt - 2])
                    for kc in range(32):
                        for nn in range(2):
                            r = emit(
                                "tensor",
                                lambda e, p=pv[kt % 2], kc=kc, nn=nn, ts=ts:
                                e.matmul(p[:, nn * 512:(nn + 1) * 512],
                                         xfm[:, kc, ts * 128:(ts + 1) * 128],
                                         wv_bf[:, kc, nn * 512:(nn + 1) * 512],
                                         start=(kc == 0), stop=(kc == 31)),
                                inc=(kc == 31 and nn == 1))
                    vmm[ts if H == 0 else ts + 8] = r
                    W("vector", sP, r)
                    if kt >= 2:
                        WD("vector", v_spill[kt - 2])
                    v_evac[kt] = emit("vector", lambda e, p=pv[kt % 2], kt=kt:
                                      e.tensor_copy(vst[kt % 2][:, :], p[:, :]), inc=True)
                    W("gpsimd", sV, v_evac[kt])
                    v_spill[kt] = dma("gpsimd", 3 + kt % 2, v_sp[:, kt, :], vst[kt % 2][:, :])
            drain(); b.flush()

        for H in range(2):
            with ExitStack() as esB:
                attn = esB.enter_context(
                    nc.sbuf_tensor(f"attnB{H}", [128, 32, 1024], BF16, side="right"))

                with ExitStack() as esB1:
                    xfm = esB1.enter_context(nc.sbuf_tensor(f"xfmB{H}", [128, 32, 1024], BF16))
                    kbuf = [esB1.enter_context(nc.sbuf_tensor(f"kbB{H}_{i}", [128, 2048], BF16)) for i in range(2)]
                    vbuf = [esB1.enter_context(nc.sbuf_tensor(f"vbB{H}_{i}", [128, 2048], BF16)) for i in range(2)]
                    wqb = [esB1.enter_context(nc.sbuf_tensor(f"wqB{H}_{i}", [128, DM], BF16)) for i in range(2)]
                    Qb = [esB1.enter_context(nc.sbuf_tensor(f"QbB{H}_{i}", [128, 512], BF16)) for i in range(2)]
                    Eb = [esB1.enter_context(nc.sbuf_tensor(f"EbB{H}_{i}", [128, 1024], BF16)) for i in range(3)]
                    rden = esB1.enter_context(nc.sbuf_tensor(f"rdenB{H}", [1, 512], F32))
                    rdbc = esB1.enter_context(nc.sbuf_tensor(f"rdbcB{H}", [128, 512], F32))
                    pq = [esB1.enter_context(nc.psum_tensor(f"pqB{H}_{i}", [128, 512], F32)) for i in range(2)]
                    ps = [esB1.enter_context(nc.psum_tensor(f"psB{H}_{i}", [128, 1024], F32)) for i in range(2)]
                    pd = esB1.enter_context(nc.psum_tensor(f"pdB{H}", [1, 512], F32))
                    ppv = esB1.enter_context(nc.psum_tensor(f"ppvB{H}", [128, 512], F32))

                    c_xfm = dma("sync", 2, xfm[:, :, :], xfm_sp[H])
                    kv_load = {}
                    wq_load = {}
                    q_done = {}; qev = {}; s_done = {}; pv_done = {}; d15 = {}
                    bc_done = {}; recip_done = {}; norm_done = {}; exp_done = {}

                    def q_mm_lams(g):
                        h, qc = divmod(g, 2)
                        lams = []
                        for kc in range(32):
                            lams.append(lambda e, kc=kc, h=h, qc=qc, g=g:
                                        e.matmul(pq[g % 2][:, :],
                                                 wqb[h % 2][:, kc * 128:(kc + 1) * 128],
                                                 xfm[:, kc, qc * 512:(qc + 1) * 512],
                                                 start=(kc == 0), stop=(kc == 31)))
                        return lams

                    def mk_s(kt, g, hv):
                        return lambda e: e.matmul(
                            ps[(kt // 2) % 2][:, (kt % 2) * 512:(kt % 2 + 1) * 512],
                            kbuf[hv % 2][:, kt * 128:(kt + 1) * 128],
                            Qb[g % 2][:, :], start=True, stop=True)

                    def emit_s03(g):
                        hvg = g // 8
                        W("tensor", sV, qev[g])
                        WD("tensor", kv_load[hvg])
                        if g >= 1:
                            W("tensor", sA, exp_done[(g - 1, 7)])
                        for kt in range(4):
                            s_done[(g, kt)] = emit("tensor", mk_s(kt, g, hvg),
                                                   inc=(kt % 2 == 1))

                    def load_kv(hv):
                        if hv >= 2:
                            lh = (hv - 2) * 4 + 3
                            W("sync", sP, pv_done[(lh * 2 + 1, 15)])
                        dma("sync", 3 + hv % 2, kbuf[hv % 2][:, :], k_sp[:, hv, :])
                        kv_load[hv] = dma("sync", 3 + hv % 2, vbuf[hv % 2][:, :],
                                          v_sp[:, :, hv * 128:(hv + 1) * 128])

                    load_kv(0)
                    wq_load[0] = dma("gpsimd", 5, wqb[0][:, :], wqt_d[0])
                    wq_load[1] = dma("gpsimd", 6, wqb[1][:, :], wqt_d[1])
                    WD("tensor", wq_load[0])
                    WD("tensor", c_xfm)
                    for i, lam in enumerate(q_mm_lams(0)):
                        q_done[0] = emit("tensor", lam, inc=(i == 31))
                    W("vector", sP, q_done[0])
                    qev[0] = emit("vector", lambda e: e.tensor_copy(Qb[0][:, :], pq[0][:, :]),
                                  inc=True)
                    emit_s03(0)

                    for h in range(32):
                        hv = h // 4
                        if h % 4 == 3 and hv + 1 < 8:
                            load_kv(hv + 1)
                        for qc in range(2):
                            g = h * 2 + qc
                            if qc == 1 and h + 2 < 32:
                                W("gpsimd", sP, q_done[2 * h + 1])
                                wq_load[h + 2] = dma("gpsimd", 5 + h % 2, wqb[h % 2][:, :],
                                                     wqt_d[h + 2])
                            nxt = q_mm_lams(g + 1) if g + 1 < 64 else None
                            nxt_waits = []
                            if nxt is not None:
                                nh = (g + 1) // 2
                                if nh != h:
                                    nxt_waits.append(("D", wq_load[nh]))
                                if g >= 1:
                                    nxt_waits.append(("V", bc_done[g - 1]))

                            def mk_d(kt, g=g):
                                return lambda e: e.matmul(
                                    pd[:, :], onec[:, :],
                                    Eb[(kt // 2) % 3][:, (kt % 2) * 512:(kt % 2 + 1) * 512],
                                    start=(kt == 0), stop=(kt == 15))

                            def mk_pv(kt, g=g, hv=hv):
                                return lambda e: e.matmul(
                                    ppv[:, :], vbuf[hv % 2][:, kt * 128:(kt + 1) * 128],
                                    Eb[(kt // 2) % 3][:, (kt % 2) * 512:(kt % 2 + 1) * 512],
                                    start=(kt == 0), stop=(kt == 15))

                            if g >= 1:
                                W("tensor", sV, recip_done[g - 1])
                                W("tensor", sV, norm_done[g - 1])
                            for kt2 in range(8):
                                W("scalar", sP, s_done[(g, kt2 * 2 + 1)])
                                P = g * 8 + kt2 - 3
                                if P >= 0:
                                    gp, ktp = divmod(P, 8)
                                    W("scalar", sP, pv_done[(gp, ktp * 2 + 1)])
                                exp_done[(g, kt2)] = emit(
                                    "scalar",
                                    lambda e, kt2=kt2: e.activation(
                                        Eb[kt2 % 3][:, :], ps[kt2 % 2][:, :],
                                        AF.Exp, scale=SCALE), inc=True)
                                W("tensor", sA, exp_done[(g, kt2)])
                                for half in range(2):
                                    kt = kt2 * 2 + half
                                    r = emit("tensor", mk_d(kt), inc=(kt == 15))
                                    if kt == 15:
                                        d15[g] = r
                                    pv_done[(g, kt)] = emit("tensor", mk_pv(kt), inc=True)
                                if nxt is not None:
                                    for w in nxt_waits:
                                        if w[0] == "D":
                                            WD("tensor", w[1])
                                        else:
                                            W("tensor", sV, w[1])
                                    nxt_waits = []
                                    for i in range(4):
                                        lam = nxt[kt2 * 4 + i]
                                        q_done[g + 1] = emit("tensor", lam,
                                                             inc=(kt2 == 7 and i == 3))
                                if kt2 < 6:
                                    kt = kt2 * 2 + 4
                                    s_done[(g, kt)] = emit("tensor", mk_s(kt, g, hv),
                                                           inc=False)
                                    s_done[(g, kt + 1)] = emit("tensor", mk_s(kt + 1, g, hv),
                                                               inc=True)
                            if nxt is not None:
                                W("vector", sP, q_done[g + 1])
                                qev[g + 1] = emit(
                                    "vector",
                                    lambda e, g=g: e.tensor_copy(Qb[(g + 1) % 2][:, :],
                                                                 pq[(g + 1) % 2][:, :]),
                                    inc=True)
                            W("vector", sP, d15[g])
                            recip_done[g] = emit("vector", lambda e:
                                                 e.reciprocal(rden[:, :], pd[:, :]), inc=True)
                            if nxt is not None:
                                emit_s03(g + 1)
                            W("tensor", sV, recip_done[g])
                            bcmm = emit("tensor", lambda e, g=g:
                                        e.matmul(pq[g % 2][:, :], oner[:, :], rden[:, :],
                                                 start=True, stop=True), inc=True)
                            W("vector", sP, bcmm)
                            bc_done[g] = emit("vector", lambda e, g=g:
                                              e.tensor_copy(rdbc[:, :], pq[g % 2][:, :]),
                                              inc=True)
                            W("vector", sP, pv_done[(g, 15)])
                            norm_done[g] = emit(
                                "vector",
                                lambda e, h=h, qc=qc:
                                e.tensor_tensor(attn[:, h, qc * 512:(qc + 1) * 512],
                                                ppv[:, :], rdbc[:, :], ALU.mult), inc=True)
                    drain(); b.flush()

                with ExitStack() as esB2:
                    atm = esB2.enter_context(nc.sbuf_tensor(f"atmB{H}", [128, 8, DM], BF16))
                    esB2i = esB2.enter_context(ExitStack())
                    wob = [esB2i.enter_context(nc.sbuf_tensor(f"woB{H}_{i}", [128, DM], BF16)) for i in range(2)]
                    woc = [esB2i.enter_context(nc.sbuf_tensor(f"wocB{H}_{i}", [128, 1024], BF16)) for i in range(2)]
                    pwo = [esB2i.enter_context(nc.psum_tensor(f"pwoB{H}_{i}", [128, 512], F32)) for i in range(2)]
                    ptb = [esB2i.enter_context(nc.psum_tensor(f"ptbB{H}_{i}", [128, 512], F32)) for i in range(2)]

                    woev = {}
                    tev = {}
                    tmm = {}
                    bi = 0
                    for oc in range(32):
                        if oc >= 2:
                            W("gpsimd", sP, tmm[(oc - 2, 1)])
                        c_wo = dma("gpsimd", 1 + oc % 2, wob[oc % 2][:, :], wot_d[oc])
                        WD("tensor", c_wo)
                        for tcol in range(2):
                            if bi >= 2:
                                W("tensor", sV, woev[bi - 2])
                            for dc in range(32):
                                r = emit(
                                    "tensor",
                                    lambda e, oc=oc, tcol=tcol, dc=dc:
                                    e.matmul(pwo[tcol % 2][:, :],
                                             wob[oc % 2][:, dc * 128:(dc + 1) * 128],
                                             attn[:, dc, tcol * 512:(tcol + 1) * 512],
                                             start=(dc == 0), stop=(dc == 31)),
                                    inc=(dc == 31))
                            W("vector", sP, r)
                            woev[bi] = emit(
                                "vector", lambda e, oc=oc, tcol=tcol:
                                e.tensor_copy(woc[oc % 2][:, tcol * 512:(tcol + 1) * 512],
                                              pwo[tcol % 2][:, :]), inc=True)
                            bi += 1
                        for tb in range(2):
                            gb = oc * 2 + tb
                            if gb >= 2:
                                W("tensor", sV, tev[gb - 2])
                            W("tensor", sV, woev[bi - 1])
                            for j in range(4):
                                ts = tb * 4 + j
                                r = emit(
                                    "tensor",
                                    lambda e, oc=oc, ts=ts, j=j, tb=tb:
                                    e.matmul(ptb[tb % 2][:, j * 128:(j + 1) * 128],
                                             woc[oc % 2][:, ts * 128:(ts + 1) * 128],
                                             ident[:, :], start=True, stop=True),
                                    inc=(j == 3))
                            tmm[(oc, tb)] = r
                            W("vector", sP, r)
                            tev[gb] = emit(
                                "vector",
                                lambda e, oc=oc, tb=tb:
                                e.tensor_copy(
                                    atm[:, tb * 4:(tb + 1) * 4, oc * 128:(oc + 1) * 128],
                                    ptb[tb % 2][:, :].rearrange("p (c t) -> p c t", c=4)),
                                inc=True)
                    drain(); b.flush()
                    esB2i.close()

                    with ExitStack() as esB3:
                        xres_ = esB3.enter_context(nc.sbuf_tensor(f"xrB{H}", [128, DM], F32))
                        xres = [xres_, xres_]
                        sres = [esB3.enter_context(nc.sbuf_tensor(f"srB{H}_{i}", [128, DM], F32)) for i in range(2)]
                        r1t = [esB3.enter_context(nc.sbuf_tensor(f"r1B{H}_{i}", [128, DM], BF16, side="right")) for i in range(2)]
                        st_ = esB3.enter_context(nc.sbuf_tensor(f"stB{H}", [128, 16], F32, side="right"))

                        addv = {}; nrm = {}; r1w = {}; sq_done = {}
                        for ts in range(8):
                            tg = H * 8 + ts
                            base = 8 * (ts % 2)
                            if ts >= 1:
                                W("sync", sV, addv[ts - 1])
                                W("sync", sA, sq_done[ts - 1])
                            c_x = dma("sync", 3, xres[ts % 2][:, :],
                                      x_d[tg * 128:(tg + 1) * 128, :])
                            WD("vector", c_x)
                            if ts >= 2:
                                W("vector", sV, nrm[ts - 2])
                            addv[ts] = emit(
                                "vector", lambda e, ts=ts:
                                e.tensor_tensor(sres[ts % 2][:, :], xres[ts % 2][:, :],
                                                atm[:, ts, :], ALU.add), inc=True)
                            W("scalar", sV, addv[ts])
                            sq_d = emit("scalar", lambda e, ts=ts, base=base:
                                        e.activation(xres[ts % 2][:, :], sres[ts % 2][:, :],
                                                     AF.Square,
                                                     accum_out=st_[:, base + 1:base + 2]),
                                        inc=True)
                            sq_done[ts] = sq_d
                            sum_d = emit("vector", lambda e, ts=ts, base=base:
                                         e.reduce_sum(st_[:, base:base + 1], sres[ts % 2][:, :],
                                                      axis=AXL.X), inc=True)
                            W("vector", sA, sq_d)
                            W("vector", sV, sum_d)
                            m1 = emit("vector", lambda e, base=base:
                                      e.tensor_scalar_mul(st_[:, base + 2:base + 3],
                                                          st_[:, base:base + 1], 1.0 / DM),
                                      inc=True)
                            m2 = emit("vector", lambda e, base=base:
                                      e.tensor_scalar_mul(st_[:, base + 3:base + 4],
                                                          st_[:, base + 1:base + 2], 1.0 / DM),
                                      inc=True)
                            W("vector", sV, m2)
                            m3 = emit("vector", lambda e, base=base:
                                      e.tensor_tensor(st_[:, base + 7:base + 8],
                                                      st_[:, base + 2:base + 3],
                                                      st_[:, base + 2:base + 3], ALU.mult),
                                      inc=True)
                            W("vector", sV, m3)
                            m4 = emit("vector", lambda e, base=base:
                                      e.scalar_tensor_tensor(st_[:, base + 4:base + 5],
                                                             st_[:, base + 3:base + 4], EPS,
                                                             st_[:, base + 7:base + 8],
                                                             ALU.add, ALU.subtract), inc=True)
                            W("scalar", sV, m4)
                            sqr = emit("scalar", lambda e, base=base:
                                       e.activation(st_[:, base + 5:base + 6],
                                                    st_[:, base + 4:base + 5], AF.Sqrt),
                                       inc=True)
                            W("vector", sA, sqr)
                            rcp = emit("vector", lambda e, base=base:
                                       e.reciprocal(st_[:, base + 6:base + 7],
                                                    st_[:, base + 5:base + 6]), inc=True)
                            W("vector", sV, rcp)
                            if ts >= 2:
                                WD("vector", r1w[ts - 2])
                            nrm[ts] = emit(
                                "vector", lambda e, ts=ts, base=base:
                                e.tensor_scalar(r1t[ts % 2][:, :], sres[ts % 2][:, :],
                                                st_[:, base + 2:base + 3],
                                                st_[:, base + 6:base + 7],
                                                ALU.subtract, ALU.mult), inc=True)
                            W("gpsimd", sV, nrm[ts])
                            r1w[ts] = dma("gpsimd", 5 + ts % 2, r1_sp[tg * 128:(tg + 1) * 128, :],
                                          r1t[ts % 2][:, :])
                        drain(); b.flush()

        if dump:
            c = dma("gpsimd", 0, dbg["r1"], r1_sp[:, :])
            drain()
            b.flush()
        if dump == 1:
            drain()
            b.flush()
            return nc

        with ExitStack() as esC:
            r1fm = esC.enter_context(nc.sbuf_tensor("r1fm", [128, 32, 2048], BF16, side="right"))

            with ExitStack() as esC0:
                r1tm = [esC0.enter_context(nc.sbuf_tensor(f"r1tmC{i}", [128, DM], BF16)) for i in range(2)]
                ptb = [esC0.enter_context(nc.psum_tensor(f"ptbC{i}", [128, 512], F32)) for i in range(2)]
                tmmc = {}; tevc = {}
                bi = 0
                for ts in range(16):
                    if ts >= 2:
                        W("sync", sP, tmmc[ts - 2])
                    c_r = dma("sync", 1 + ts % 2, r1tm[ts % 2][:, :], r1_sp[ts * 128:(ts + 1) * 128, :])
                    WD("tensor", c_r)
                    for cb in range(8):
                        if bi >= 2:
                            W("tensor", sV, tevc[bi - 2])
                        for j in range(4):
                            c = cb * 4 + j
                            r = emit(
                                "tensor",
                                lambda e, pb=ptb[bi % 2], ts=ts, c=c, j=j:
                                e.matmul(pb[:, j * 128:(j + 1) * 128],
                                         r1tm[ts % 2][:, c * 128:(c + 1) * 128],
                                         ident[:, :], start=True, stop=True),
                                inc=(j == 3))
                        tmmc[ts] = r
                        W("vector", sP, r)
                        tevc[bi] = emit(
                            "vector",
                            lambda e, pb=ptb[bi % 2], cb=cb, ts=ts:
                            e.tensor_copy(r1fm[:, cb * 4:(cb + 1) * 4, ts * 128:(ts + 1) * 128],
                                          pb[:, :].rearrange("p (c t) -> p c t", c=4)),
                            inc=True)
                        bi += 1
                drain(); b.flush()

            with ExitStack() as esC1:
                qq = {m: [esC1.enter_context(nc.sbuf_tensor(f"qq{m}{i}", [128, 1024], I32))
                          for i in range(2)] for m in range(2)}
                szt = {m: [esC1.enter_context(nc.sbuf_tensor(f"szt{m}{i}", [128, 64], F32))
                           for i in range(2)] for m in range(2)}
                w1 = {m: esC1.enter_context(nc.sbuf_tensor(f"w1C{m}", [128, DM], BF16))
                      for m in range(2)}
                wT = {m: esC1.enter_context(nc.sbuf_tensor(f"wTC{m}", [128, DM], BF16))
                      for m in range(2)}
                sgb = [esC1.enter_context(nc.sbuf_tensor(f"sgbC{i}", [128, 512], BF16)) for i in range(4)]
                hst = [esC1.enter_context(nc.sbuf_tensor(f"hstC{i}", [128, 512], BF16)) for i in range(4)]
                ptb = [esC1.enter_context(nc.psum_tensor(f"ptbC1{i}", [128, 512], F32)) for i in range(2)]
                pg = [esC1.enter_context(nc.psum_tensor(f"pgC{i}", [128, 512], F32)) for i in range(2)]
                pu = [esC1.enter_context(nc.psum_tensor(f"puC{i}", [128, 512], F32)) for i in range(4)]

                qsrc = {0: qg_d, 1: qu_d}
                ssrc = {0: sg_d, 1: su_d}
                zsrc = {0: zg_d, 1: zu_d}
                qload = {}; szload = {}; deq = {}; mmt = {}; tevc = {}
                mmg = {}; silu_d = {}; hmul = {}; hw = {}
                HWCH = {0: 7, 1: 8, 2: 9, 3: 0}

                def emit_hmul(ic):
                    rows_ = slice(ic * 128, (ic + 1) * 128)
                    for tcol in range(4):
                        W("vector", sP, mmg[(ic, 1, tcol)])
                        W("vector", sA, silu_d[(ic, tcol)])
                        if (ic - 1, tcol) in hw:
                            WD("vector", hw[(ic - 1, tcol)])
                        hmul[(ic, tcol)] = emit(
                            "vector",
                            lambda e, tcol=tcol:
                            e.tensor_tensor(hst[tcol][:, :], pu[tcol][:, :],
                                            sgb[tcol][:, :], ALU.mult), inc=True)
                        W("gpsimd", sV, hmul[(ic, tcol)])
                        hw[(ic, tcol)] = dma("gpsimd", HWCH[tcol],
                                             h_sp[rows_, tcol * 512:(tcol + 1) * 512],
                                             hst[tcol][:, :])

                bti = 0
                for ic in range(NIC):
                    rows = slice(ic * 128, (ic + 1) * 128)
                    for m in range(2):
                        if ic >= 2:
                            W("sync", sV, deq[(ic - 2, m, 3)])
                        dma("sync", 5 + m, szt[m][ic % 2][:, 0:32], ssrc[m][rows, :])
                        szload[(ic, m)] = dma("sync", 5 + m, szt[m][ic % 2][:, 32:64],
                                              zsrc[m][rows, :])
                        for j in range(2):
                            if ic >= 1:
                                W("sync", sV, deq[(ic - 1, m, j + 2)])
                            qload[(ic, m, j)] = dma(
                                "sync", 1 + m * 2 + j % 2, qq[m][j % 2][:, :],
                                qsrc[m][rows, j * 1024:(j + 1) * 1024])
                    for m in range(2):
                        if ic >= 1:
                            W("vector", sP, mmt[(ic - 1, m)])
                        WD("vector", szload[(ic, m)])
                        for j in range(4):
                            WD("vector", qload[(ic, m, j)])
                            for g8 in range(8):
                                g = j * 8 + g8
                                deq[(ic, m, j)] = emit(
                                    "vector",
                                    lambda e, m=m, j=j, g=g, g8=g8, ic=ic:
                                    e.tensor_scalar(w1[m][:, g * 128:(g + 1) * 128],
                                                    qq[m][j % 2][:, g8 * 128:(g8 + 1) * 128],
                                                    szt[m][ic % 2][:, 32 + g:33 + g],
                                                    szt[m][ic % 2][:, g:g + 1],
                                                    ALU.subtract, ALU.mult),
                                    inc=(g8 == 7))
                            if j + 2 <= 3:
                                W("sync", sV, deq[(ic, m, j)])
                                qload[(ic, m, j + 2)] = dma(
                                    "sync", 1 + m * 2 + j % 2, qq[m][j % 2][:, :],
                                    qsrc[m][rows, (j + 2) * 1024:(j + 3) * 1024])
                    for m in range(2):
                        if ic >= 1:
                            W("vector", sP, mmg[(ic - 1, m, 3)])
                        for cb in range(8):
                            if bti >= 2:
                                W("tensor", sV, tevc[bti - 2])
                            W("tensor", sV, deq[(ic, m, cb // 2)])
                            for j in range(4):
                                g = cb * 4 + j
                                r = emit(
                                    "tensor",
                                    lambda e, pb=ptb[bti % 2], m=m, g=g, j=j:
                                    e.matmul(pb[:, j * 128:(j + 1) * 128],
                                             w1[m][:, g * 128:(g + 1) * 128],
                                             ident[:, :], start=True, stop=True),
                                    inc=(j == 3))
                            W("vector", sP, r)
                            tevc[bti] = emit(
                                "vector",
                                lambda e, pb=ptb[bti % 2], m=m, cb=cb:
                                e.tensor_copy(wT[m][:, cb * 512:(cb + 1) * 512], pb[:, :]),
                                inc=True)
                            bti += 1
                        mmt[(ic, m)] = r
                    wT_ready = tevc[bti - 1]
                    if ic >= 1:
                        emit_hmul(ic - 1)
                    for tcol in range(4):
                        for m in range(2):
                            pp = pg[tcol % 2] if m == 0 else pu[tcol]
                            W("tensor", sV, wT_ready)
                            if m == 0:
                                prev = silu_d.get((ic, tcol - 2), silu_d.get((ic - 1, tcol + 2)))
                                if prev:
                                    W("tensor", sA, prev)
                            else:
                                if (ic - 1, tcol) in hmul:
                                    W("tensor", sV, hmul[(ic - 1, tcol)])
                            for kc in range(32):
                                r = emit(
                                    "tensor",
                                    lambda e, pp=pp, m=m, kc=kc, tcol=tcol:
                                    e.matmul(pp[:, :],
                                             wT[m][:, kc * 128:(kc + 1) * 128],
                                             r1fm[:, kc, tcol * 512:(tcol + 1) * 512],
                                             start=(kc == 0), stop=(kc == 31)),
                                    inc=(kc == 31))
                            mmg[(ic, m, tcol)] = r
                        W("scalar", sP, mmg[(ic, 0, tcol)])
                        if (ic - 1, tcol) in hmul:
                            W("scalar", sV, hmul[(ic - 1, tcol)])
                        silu_d[(ic, tcol)] = emit(
                            "scalar",
                            lambda e, tcol=tcol: e.activation(sgb[tcol][:, :],
                                                              pg[tcol % 2][:, :], AF.Silu),
                            inc=True)
                emit_hmul(NIC - 1)
                drain(); b.flush()

        with ExitStack() as esC2:
            sdt = esC2.enter_context(nc.sbuf_tensor("sdtC", [128, 32, NIC], F32))
            zdt = esC2.enter_context(nc.sbuf_tensor("zdtC", [128, 32, NIC], F32))
            qsl = [esC2.enter_context(nc.sbuf_tensor(f"qslC{i}", [128, DM], I32)) for i in range(2)]
            w1d = [esC2.enter_context(nc.sbuf_tensor(f"w1dC{i}", [128, DM], BF16)) for i in range(2)]
            stg = [esC2.enter_context(nc.sbuf_tensor(f"stgC{i}", [128, DM], BF16)) for i in range(2)]
            ptb = [esC2.enter_context(nc.psum_tensor(f"ptbC2{i}", [128, 512], F32)) for i in range(2)]

            dma("sync", 5, sdt[:, :, :], sdt_d)
            c_sz = dma("sync", 5, zdt[:, :, :], zdt_d)
            WD("vector", c_sz)
            ql = {}; dq = {}; mt = {}; tev2 = {}; stw = {}
            bti = 0
            for gi in range(NIC):
                if gi >= 2:
                    W("sync", sV, dq[gi - 2])
                ql[gi] = dma("sync", 1 + gi % 2, qsl[gi % 2][:, :], qdt_d[gi])
                WD("vector", ql[gi])
                if gi >= 2:
                    W("vector", sP, mt[gi - 2])
                for hoc in range(32):
                    dq[gi] = emit(
                        "vector",
                        lambda e, gi=gi, hoc=hoc:
                        e.tensor_scalar(w1d[gi % 2][:, hoc * 128:(hoc + 1) * 128],
                                        qsl[gi % 2][:, hoc * 128:(hoc + 1) * 128],
                                        zdt[:, hoc, gi:gi + 1], sdt[:, hoc, gi:gi + 1],
                                        ALU.subtract, ALU.mult),
                        inc=(hoc == 31))
                W("tensor", sV, dq[gi])
                for cb in range(8):
                    if bti >= 2:
                        W("tensor", sA, tev2[bti - 2])
                    for j in range(4):
                        hoc = cb * 4 + j
                        r = emit(
                            "tensor",
                            lambda e, pb=ptb[bti % 2], gi=gi, hoc=hoc, j=j:
                            e.matmul(pb[:, j * 128:(j + 1) * 128],
                                     w1d[gi % 2][:, hoc * 128:(hoc + 1) * 128],
                                     ident[:, :], start=True, stop=True),
                            inc=(j == 3))
                    mt[gi] = r
                    W("scalar", sP, r)
                    if cb == 0 and gi >= 2:
                        WD("scalar", stw[gi - 2])
                    tev2[bti] = emit(
                        "scalar",
                        lambda e, pb=ptb[bti % 2], gi=gi, cb=cb:
                        e.copy(stg[gi % 2][:, cb * 512:(cb + 1) * 512], pb[:, :]),
                        inc=True)
                    bti += 1
                W("gpsimd", sA, tev2[bti - 1])
                stw[gi] = dma("gpsimd", 3 + gi % 2, wdt_sp[gi * 128:(gi + 1) * 128, :], stg[gi % 2][:, :])
            drain(); b.flush()

        if dump == 2:
            dma("gpsimd", 0, dbg["h"], h_sp[:, 0:512])
            dma("gpsimd", 0, dbg["wdt"], wdt_sp[0:512, :])
            drain()
            b.flush()
            return nc

        with ExitStack() as esD:
            ht = esD.enter_context(nc.sbuf_tensor("htD", [128, NIC, 512], BF16))
            ftm = esD.enter_context(nc.sbuf_tensor("ftmD", [128, 4, DM], F32))
            wdb = [esD.enter_context(nc.sbuf_tensor(f"wdbD{i}", [128, 1024], BF16)) for i in range(4)]
            r1q = esD.enter_context(nc.sbuf_tensor("r1qD", [128, 4, 1024], BF16))
            obuf = esD.enter_context(nc.sbuf_tensor("obufD", [128, DM], F32))
            st_ = esD.enter_context(nc.sbuf_tensor("stD", [128, 16], F32))
            pf = [esD.enter_context(nc.psum_tensor(f"pfD{i}", [128, 1024], F32)) for i in range(4)]

            WDCH = {0: 2, 1: 3, 2: 6, 3: 7}
            hload = {}; dmm = {}; fev = {}; r1l = {}; outw = {}
            for tm in range(4):
                tc = slice(tm * 512, (tm + 1) * 512)

                def issue_h(icl, tm=tm, tc=tc):
                    if tm >= 1:
                        W("gpsimd", sP, dmm[(tm - 1, 3, icl)])
                    hload[(tm, icl)] = dma("gpsimd", 1, ht[:, icl, :],
                                           h_sp[icl * 128:(icl + 1) * 128, tc])

                for hoq in range(4):
                    hc = slice(hoq * 1024, (hoq + 1) * 1024)
                    for ts in range(4):
                        prevf = fev.get((tm, hoq - 1, ts), fev.get((tm - 1, 3, ts)))
                        if prevf:
                            W("sync", sV, prevf)
                        r1l[(tm, hoq, ts)] = dma(
                            "sync", 4 + ts // 2, r1q[:, ts, :],
                            r1_sp[tm * 512 + ts * 128: tm * 512 + (ts + 1) * 128, hc])
                    for ic in range(NIC):
                        par = ic % 4
                        gmm = dmm.get((tm, hoq, ic - 4), dmm.get((tm, hoq - 1, NIC - 4 + ic),
                                                                 dmm.get((tm - 1, 3, NIC - 4 + ic))))
                        if gmm:
                            W("sync", sP, gmm)
                        c_w = dma("sync", WDCH[par], wdb[par][:, :],
                                  wdt_sp[ic * 128:(ic + 1) * 128, hc])
                        WD("tensor", c_w)
                        if hoq == 0:
                            if ic == 0:
                                for k in range(8):
                                    issue_h(k)
                            elif ic + 7 < NIC:
                                issue_h(ic + 7)
                            WD("tensor", hload[(tm, min(ic + 7, NIC - 1))])
                        if ic == 0:
                            for ts in range(4):
                                prevf = fev.get((tm, hoq - 1, ts), fev.get((tm - 1, 3, ts)))
                                if prevf:
                                    W("tensor", sV, prevf)
                        for ts in range(4):
                            for nn in range(2):
                                r = emit(
                                    "tensor",
                                    lambda e, par=par, ts=ts, nn=nn, ic=ic:
                                    e.matmul(pf[ts][:, nn * 512:(nn + 1) * 512],
                                             ht[:, ic, ts * 128:(ts + 1) * 128],
                                             wdb[par][:, nn * 512:(nn + 1) * 512],
                                             start=(ic == 0), stop=(ic == NIC - 1)),
                                    inc=(ts == 3 and nn == 1))
                        dmm[(tm, hoq, ic)] = r
                    for ts in range(4):
                        W("vector", sP, dmm[(tm, hoq, NIC - 1)])
                        WD("vector", r1l[(tm, hoq, (ts // 2) * 2 + 1)])
                        fev[(tm, hoq, ts)] = emit(
                            "vector",
                            lambda e, ts=ts, hoq=hoq:
                            e.tensor_tensor(ftm[:, ts, hoq * 1024:(hoq + 1) * 1024],
                                            pf[ts][:, :], r1q[:, ts, :], ALU.add), inc=True)
                for ts in range(4):
                    tg = tm * 4 + ts
                    base = 8 * (ts % 2)
                    W("scalar", sV, fev[(tm, 3, ts)])
                    if tg >= 1:
                        WD("scalar", outw[tg - 1])
                    sq_d = emit("scalar", lambda e, ts=ts, base=base:
                                e.activation(obuf[:, :], ftm[:, ts, :], AF.Square,
                                             accum_out=st_[:, base + 1:base + 2]), inc=True)
                    W("vector", sV, fev[(tm, 3, ts)])
                    sum_d = emit("vector", lambda e, ts=ts, base=base:
                                 e.reduce_sum(st_[:, base:base + 1], ftm[:, ts, :],
                                              axis=AXL.X), inc=True)
                    W("vector", sA, sq_d)
                    W("vector", sV, sum_d)
                    m1 = emit("vector", lambda e, base=base:
                              e.tensor_scalar_mul(st_[:, base + 2:base + 3],
                                                  st_[:, base:base + 1], 1.0 / DM), inc=True)
                    m2 = emit("vector", lambda e, base=base:
                              e.tensor_scalar_mul(st_[:, base + 3:base + 4],
                                                  st_[:, base + 1:base + 2], 1.0 / DM), inc=True)
                    W("vector", sV, m2)
                    m3 = emit("vector", lambda e, base=base:
                              e.tensor_tensor(st_[:, base + 7:base + 8],
                                              st_[:, base + 2:base + 3],
                                              st_[:, base + 2:base + 3], ALU.mult), inc=True)
                    W("vector", sV, m3)
                    m4 = emit("vector", lambda e, base=base:
                              e.scalar_tensor_tensor(st_[:, base + 4:base + 5],
                                                     st_[:, base + 3:base + 4], EPS,
                                                     st_[:, base + 7:base + 8],
                                                     ALU.add, ALU.subtract), inc=True)
                    W("scalar", sV, m4)
                    sqr = emit("scalar", lambda e, base=base:
                               e.activation(st_[:, base + 5:base + 6],
                                            st_[:, base + 4:base + 5], AF.Sqrt), inc=True)
                    W("vector", sA, sqr)
                    rcp = emit("vector", lambda e, base=base:
                               e.reciprocal(st_[:, base + 6:base + 7],
                                            st_[:, base + 5:base + 6]), inc=True)
                    W("vector", sV, rcp)
                    W("vector", sA, sq_d)
                    if tg >= 1:
                        WD("vector", outw[tg - 1])
                    nrm = emit("vector", lambda e, ts=ts, base=base:
                               e.tensor_scalar(obuf[:, :], ftm[:, ts, :],
                                               st_[:, base + 2:base + 3],
                                               st_[:, base + 6:base + 7],
                                               ALU.subtract, ALU.mult), inc=True)
                    W("gpsimd", sV, nrm)
                    outw[tg] = dma("gpsimd", 8, out_d[tg * 128:(tg + 1) * 128, :], obuf[:, :])
            drain()
            b.flush()

    return nc


def _host_prep(inputs):
    Wq = np.asarray(inputs["L0_attn_Wq"])
    Wk = np.asarray(inputs["L0_attn_Wk"])
    Wv = np.asarray(inputs["L0_attn_Wv"])
    Wo = np.asarray(inputs["L0_attn_Wo"])
    qd = np.asarray(inputs["L0_ffn_qw_down"])
    sd = np.asarray(inputs["L0_ffn_s_down"])
    zd = np.asarray(inputs["L0_ffn_z_down"])
    common = {
        "wqt_d": np.ascontiguousarray(Wq.T.reshape(32, 128, 32, 128).transpose(2, 1, 0, 3)),
        "wkt_d": np.ascontiguousarray(Wk.T.reshape(32, 128, 1024).transpose(1, 0, 2)),
        "wvt_d": np.ascontiguousarray(Wv.T.reshape(32, 128, 1024).transpose(1, 0, 2)),
        "wot_d": np.ascontiguousarray(Wo.T.reshape(32, 128, 32, 128).transpose(2, 1, 0, 3)),
        "qg_d": np.ascontiguousarray(inputs["L0_ffn_qw_gate"]),
        "qu_d": np.ascontiguousarray(inputs["L0_ffn_qw_up"]),
        "sg_d": np.ascontiguousarray(inputs["L0_ffn_s_gate"]),
        "zg_d": np.ascontiguousarray(inputs["L0_ffn_z_gate"]),
        "su_d": np.ascontiguousarray(inputs["L0_ffn_s_up"]),
        "zu_d": np.ascontiguousarray(inputs["L0_ffn_z_up"]),
        "qdt_d": np.ascontiguousarray(qd.reshape(32, 128, 86, 128).transpose(2, 1, 0, 3)),
        "sdt_d": np.ascontiguousarray(sd.reshape(32, 128, 86).transpose(1, 0, 2)),
        "zdt_d": np.ascontiguousarray(zd.reshape(32, 128, 86).transpose(1, 0, 2)),
        "ident_d": np.eye(128, dtype=bfloat16),
        "onec_d": np.ones((128, 1), dtype=bfloat16),
        "oner_d": np.ones((1, 128), dtype=np.float32),
    }
    x = np.asarray(inputs["L0_attn_x"])
    maps = []
    for core in range(8):
        m = dict(common)
        m["x_d"] = np.ascontiguousarray(x[core])
        maps.append(m)
    return maps


def kernel(**inputs):
    maps = _host_prep(inputs)
    nc = build_nc(dump=0)
    res = run_bass_kernel_spmd(nc, maps, core_ids=list(range(8)))
    out = np.stack([np.asarray(res.results[c]["out_d"]) for c in range(8)], axis=0)
    return out.astype(np.float32)


# version 11
# speedup vs baseline: 1.0696x; 1.0687x over previous
import math
from contextlib import ExitStack

import numpy as np
import ml_dtypes

import concourse.bass as bass
import concourse.mybir as mybir
from concourse.bass_utils import run_bass_kernel_spmd

BF16 = mybir.dt.bfloat16
F32 = mybir.dt.float32
I32 = mybir.dt.int32
AF = mybir.ActivationFunctionType
ALU = mybir.AluOpType
AXL = mybir.AxisListType
bfloat16 = ml_dtypes.bfloat16

T = 2048
DM = 4096
NIC = 86
II = 11008
SCALE = 1.0 / math.sqrt(128.0)
EPS = 1e-5
ENGINES = ("sync", "gpsimd", "tensor", "scalar", "vector")


class Ctr:
    __slots__ = ("v",)

    def __init__(self):
        self.v = 0


class Builder:
    def __init__(self, nc):
        self.nc = nc
        self.streams = {k: [] for k in ENGINES}

    def op(self, eng, fn):
        self.streams[eng].append(fn)

    def wait(self, eng, sem, val):
        if val:
            self.streams[eng].append(lambda e, s=sem, v=val: e.wait_ge(s, v))

    def flush(self):
        nc = self.nc
        streams = self.streams
        with nc.Block() as blk:
            for name in ENGINES:
                ops = streams[name]

                def runner(e, ops=ops):
                    for f in ops:
                        f(e)

                getattr(blk, name)(runner)
        self.streams = {k: [] for k in ENGINES}


def build_nc(dump=0):
    nc = bass.Bass("TRN2", target_bir_lowering=False)

    x_d = nc.dram_tensor("x_d", [T, DM], F32, kind="ExternalInput").ap()
    wqt_d = nc.dram_tensor("wqt_d", [32, 128, 32, 128], F32, kind="ExternalInput").ap()
    wkt_d = nc.dram_tensor("wkt_d", [128, 32, 1024], F32, kind="ExternalInput").ap()
    wvt_d = nc.dram_tensor("wvt_d", [128, 32, 1024], F32, kind="ExternalInput").ap()
    wot_d = nc.dram_tensor("wot_d", [32, 128, 32, 128], F32, kind="ExternalInput").ap()
    qg_d = nc.dram_tensor("qg_d", [II, DM], I32, kind="ExternalInput").ap()
    qu_d = nc.dram_tensor("qu_d", [II, DM], I32, kind="ExternalInput").ap()
    sg_d = nc.dram_tensor("sg_d", [II, 32], F32, kind="ExternalInput").ap()
    zg_d = nc.dram_tensor("zg_d", [II, 32], F32, kind="ExternalInput").ap()
    su_d = nc.dram_tensor("su_d", [II, 32], F32, kind="ExternalInput").ap()
    zu_d = nc.dram_tensor("zu_d", [II, 32], F32, kind="ExternalInput").ap()
    qdt_d = nc.dram_tensor("qdt_d", [NIC, 128, 32, 128], I32, kind="ExternalInput").ap()
    sdt_d = nc.dram_tensor("sdt_d", [128, 32, NIC], F32, kind="ExternalInput").ap()
    zdt_d = nc.dram_tensor("zdt_d", [128, 32, NIC], F32, kind="ExternalInput").ap()
    ident_d = nc.dram_tensor("ident_d", [128, 128], BF16, kind="ExternalInput").ap()
    onec_d = nc.dram_tensor("onec_d", [128, 1], BF16, kind="ExternalInput").ap()
    oner_d = nc.dram_tensor("oner_d", [1, 128], F32, kind="ExternalInput").ap()
    out_d = nc.dram_tensor("out_d", [T, DM], F32, kind="ExternalOutput").ap()

    xfm_sp = nc.dram_tensor("xfm_sp", [2, 128, 32, 1024], BF16).ap()
    k_sp = nc.dram_tensor("k_sp", [128, 8, 2048], BF16).ap()
    v_sp = nc.dram_tensor("v_sp", [128, 16, 1024], BF16).ap()
    r1_sp = nc.dram_tensor("r1_sp", [T, DM], BF16).ap()
    h_sp = nc.dram_tensor("h_sp", [II, T], BF16).ap()
    wdt_sp = nc.dram_tensor("wdt_sp", [II, DM], BF16).ap()
    dbg = {}
    if dump:
        dbg["r1"] = nc.dram_tensor("dbg_r1", [T, DM], F32, kind="ExternalOutput").ap()
    if dump >= 2:
        dbg["h"] = nc.dram_tensor("dbg_h", [II, 512], F32, kind="ExternalOutput").ap()
        dbg["wdt"] = nc.dram_tensor("dbg_wdt", [512, DM], F32, kind="ExternalOutput").ap()

    es = ExitStack()
    with es:
        NDMA = 10
        dsem = [es.enter_context(nc.semaphore(f"d{i}")) for i in range(NDMA)]
        dctr = [Ctr() for _ in range(NDMA)]
        sP = es.enter_context(nc.semaphore("sP"))
        sA = es.enter_context(nc.semaphore("sA"))
        sV = es.enter_context(nc.semaphore("sV"))
        cP, cA, cV = Ctr(), Ctr(), Ctr()
        SEM = {"tensor": (sP, cP), "scalar": (sA, cA), "vector": (sV, cV)}

        ident = es.enter_context(nc.sbuf_tensor("ident", [128, 128], BF16))
        onec = es.enter_context(nc.sbuf_tensor("onec", [128, 1], BF16))
        oner = es.enter_context(nc.sbuf_tensor("oner", [1, 128], F32))

        b = Builder(nc)
        W = b.wait

        def dma(eng, ch, out, in_):
            s, c = dsem[ch], dctr[ch]
            c.v += 16
            b.op(eng, lambda e, o=out, i=in_, s=s: e.dma_start(out=o, in_=i).then_inc(s, 16))
            return (s, c.v)

        def WD(eng, tok):
            if tok:
                b.wait(eng, tok[0], tok[1])

        def emit(eng, fn, inc=False):
            s, c = SEM[eng]
            if inc:
                c.v += 1
                b.op(eng, lambda e, f=fn, s=s: f(e).then_inc(s, 1))
                return c.v
            b.op(eng, fn)
            return None

        def drain():
            for i in range(NDMA):
                if dctr[i].v:
                    W("gpsimd", dsem[i], dctr[i].v)
                    W("sync", dsem[i], dctr[i].v)

        dma("sync", 0, ident[:, :], ident_d)
        dma("sync", 0, onec[:, :], onec_d)
        c_consts = dma("sync", 0, oner[:, :], oner_d)
        for eng in ("tensor", "vector", "scalar", "gpsimd"):
            WD(eng, c_consts)
        drain(); b.flush()

        with ExitStack() as esA:
            wk_bf = esA.enter_context(nc.sbuf_tensor("wk_bf", [128, 32, 1024], BF16))
            xfm = esA.enter_context(nc.sbuf_tensor("xfmA", [128, 32, 1024], BF16))
            xb = [esA.enter_context(nc.sbuf_tensor(f"xbA{i}", [128, DM], BF16)) for i in range(2)]
            kst = [esA.enter_context(nc.sbuf_tensor(f"kstA{i}", [128, 8, 512], BF16)) for i in range(2)]
            ptb = [esA.enter_context(nc.psum_tensor(f"ptbA{i}", [128, 512], F32)) for i in range(2)]
            pk = [esA.enter_context(nc.psum_tensor(f"pkA{i}", [128, 512], F32)) for i in range(2)]

            c_wk = None

            evacB = {}
            mmT_last = {}
            k_evac = {}
            k_spill = {}
            kmm = {}
            xfm_spill = {}
            bidx = 0
            for H in range(2):
                for tt in range(8):
                    tg = H * 8 + tt
                    if tg >= 2:
                        W("gpsimd", sP, mmT_last[tg - 2])
                    c_x = dma("gpsimd", 2 + tg % 2, xb[tg % 2][:, :], x_d[tg * 128:(tg + 1) * 128, :])
                    if tg == 1:
                        c_wk = dma("gpsimd", 1, wk_bf[:, :, :], wkt_d)
                    WD("tensor", c_x)
                    for cb in range(8):
                        if bidx >= 2:
                            W("tensor", sV, evacB[bidx - 2])
                        for j in range(4):
                            c = cb * 4 + j
                            r = emit("tensor",
                                     lambda e, pb=ptb[bidx % 2], xt=xb[tg % 2], c=c, j=j:
                                     e.matmul(pb[:, j * 128:(j + 1) * 128],
                                              xt[:, c * 128:(c + 1) * 128], ident[:, :],
                                              start=True, stop=True), inc=(j == 3))
                        mmT_last[tg] = r
                        W("vector", sP, r)
                        if tg == 8 and cb == 0:
                            WD("vector", xfm_spill[0])
                        evacB[bidx] = emit(
                            "vector",
                            lambda e, pb=ptb[bidx % 2], cb=cb, tt=tt:
                            e.tensor_copy(xfm[:, cb * 4:(cb + 1) * 4, tt * 128:(tt + 1) * 128],
                                          pb[:, :].rearrange("p (c t) -> p c t", c=4)),
                            inc=True)
                        bidx += 1
                xfm_ready = cV.v
                W("gpsimd", sV, xfm_ready)
                xfm_spill[H] = dma("gpsimd", 4, xfm_sp[H], xfm[:, :, :])
                WD("tensor", c_wk)
                W("tensor", sV, xfm_ready)
                for tcol in range(2):
                    gidx = H * 2 + tcol
                    for m in range(8):
                        it = gidx * 8 + m
                        pidx = it % 2
                        if it >= 2:
                            pH, prem = divmod(it - 2, 16)
                            ptc, pm = divmod(prem, 8)
                            W("tensor", sV, k_evac[(pH, ptc, pm)])
                        for kc in range(32):
                            r = emit(
                                "tensor",
                                lambda e, p=pk[pidx], m=m, kc=kc, tcol=tcol:
                                e.matmul(p[:, :],
                                         wk_bf[:, kc, m * 128:(m + 1) * 128],
                                         xfm[:, kc, tcol * 512:(tcol + 1) * 512],
                                         start=(kc == 0), stop=(kc == 31)),
                                inc=(kc == 31))
                        kmm[(H, tcol, m)] = r
                        W("vector", sP, r)
                        if m == 0 and gidx >= 2:
                            WD("vector", k_spill[gidx - 2])
                        k_evac[(H, tcol, m)] = emit(
                            "vector",
                            lambda e, p=pk[pidx], m=m, tcol=tcol:
                            e.tensor_copy(kst[tcol % 2][:, m, :], p[:, :]), inc=True)
                    W("gpsimd", sV, k_evac[(H, tcol, 7)])
                    k_spill[gidx] = dma(
                        "gpsimd", 5 + gidx % 2,
                        k_sp[:, :, gidx * 512:(gidx + 1) * 512],
                        kst[tcol % 2][:, :, :])
            last_kmm = kmm[(1, 1, 7)]
            drain(); b.flush()

        with ExitStack() as esA:
            wv_bf = esA.enter_context(nc.sbuf_tensor("wv_bf", [128, 32, 1024], BF16))
            xfm = esA.enter_context(nc.sbuf_tensor("xfmV", [128, 32, 1024], BF16))
            vst = [esA.enter_context(nc.sbuf_tensor(f"vstA{i}", [128, 1024], BF16)) for i in range(2)]
            pv = [esA.enter_context(nc.psum_tensor(f"pvA{i}", [128, 1024], F32)) for i in range(2)]

            c_wv = dma("gpsimd", 1, wv_bf[:, :, :], wvt_d)
            v_evac = {}
            v_spill = {}
            vmm = {}
            for H in range(2):
                if H == 1:
                    W("sync", sP, vmm[7])
                c_xfm = dma("sync", 2, xfm[:, :, :], xfm_sp[H])
                WD("tensor", c_xfm)
                WD("tensor", c_wv)
                for ts in range(8):
                    kt = H * 8 + ts
                    if kt >= 2:
                        W("tensor", sV, v_evac[kt - 2])
                    for kc in range(32):
                        for nn in range(2):
                            r = emit(
                                "tensor",
                                lambda e, p=pv[kt % 2], kc=kc, nn=nn, ts=ts:
                                e.matmul(p[:, nn * 512:(nn + 1) * 512],
                                         xfm[:, kc, ts * 128:(ts + 1) * 128],
                                         wv_bf[:, kc, nn * 512:(nn + 1) * 512],
                                         start=(kc == 0), stop=(kc == 31)),
                                inc=(kc == 31 and nn == 1))
                    vmm[ts if H == 0 else ts + 8] = r
                    W("vector", sP, r)
                    if kt >= 2:
                        WD("vector", v_spill[kt - 2])
                    v_evac[kt] = emit("vector", lambda e, p=pv[kt % 2], kt=kt:
                                      e.tensor_copy(vst[kt % 2][:, :], p[:, :]), inc=True)
                    W("gpsimd", sV, v_evac[kt])
                    v_spill[kt] = dma("gpsimd", 3 + kt % 2, v_sp[:, kt, :], vst[kt % 2][:, :])
            drain(); b.flush()

        for H in range(2):
            with ExitStack() as esB:
                attn = esB.enter_context(
                    nc.sbuf_tensor(f"attnB{H}", [128, 32, 1024], BF16, side="right"))

                with ExitStack() as esB1:
                    xfm = esB1.enter_context(nc.sbuf_tensor(f"xfmB{H}", [128, 32, 1024], BF16))
                    kbuf = [esB1.enter_context(nc.sbuf_tensor(f"kbB{H}_{i}", [128, 2048], BF16)) for i in range(2)]
                    vbuf = [esB1.enter_context(nc.sbuf_tensor(f"vbB{H}_{i}", [128, 2048], BF16)) for i in range(2)]
                    wqb = [esB1.enter_context(nc.sbuf_tensor(f"wqB{H}_{i}", [128, DM], BF16)) for i in range(2)]
                    Qb = [esB1.enter_context(nc.sbuf_tensor(f"QbB{H}_{i}", [128, 512], BF16)) for i in range(2)]
                    Eb = [esB1.enter_context(nc.sbuf_tensor(f"EbB{H}_{i}", [128, 1024], BF16)) for i in range(3)]
                    rden = esB1.enter_context(nc.sbuf_tensor(f"rdenB{H}", [1, 512], F32))
                    rdbc = esB1.enter_context(nc.sbuf_tensor(f"rdbcB{H}", [128, 512], F32))
                    pq = [esB1.enter_context(nc.psum_tensor(f"pqB{H}_{i}", [128, 512], F32)) for i in range(2)]
                    ps = [esB1.enter_context(nc.psum_tensor(f"psB{H}_{i}", [128, 1024], F32)) for i in range(2)]
                    pd = esB1.enter_context(nc.psum_tensor(f"pdB{H}", [1, 512], F32))
                    ppv = esB1.enter_context(nc.psum_tensor(f"ppvB{H}", [128, 512], F32))

                    c_xfm = dma("sync", 2, xfm[:, :, :], xfm_sp[H])
                    kv_load = {}
                    wq_load = {}
                    q_done = {}; qev = {}; s_done = {}; pv_done = {}; d15 = {}
                    bc_done = {}; recip_done = {}; norm_done = {}; exp_done = {}

                    def q_mm_lams(g):
                        h, qc = divmod(g, 2)
                        lams = []
                        for kc in range(32):
                            lams.append(lambda e, kc=kc, h=h, qc=qc, g=g:
                                        e.matmul(pq[g % 2][:, :],
                                                 wqb[h % 2][:, kc * 128:(kc + 1) * 128],
                                                 xfm[:, kc, qc * 512:(qc + 1) * 512],
                                                 start=(kc == 0), stop=(kc == 31)))
                        return lams

                    def mk_s(kt, g, hv):
                        return lambda e: e.matmul(
                            ps[(kt // 2) % 2][:, (kt % 2) * 512:(kt % 2 + 1) * 512],
                            kbuf[hv % 2][:, kt * 128:(kt + 1) * 128],
                            Qb[g % 2][:, :], start=True, stop=True)

                    def emit_s03(g):
                        hvg = g // 8
                        W("tensor", sV, qev[g])
                        WD("tensor", kv_load[hvg])
                        if g >= 1:
                            W("tensor", sA, exp_done[(g - 1, 7)])
                        for kt in range(4):
                            s_done[(g, kt)] = emit("tensor", mk_s(kt, g, hvg),
                                                   inc=(kt % 2 == 1))

                    def load_kv(hv):
                        if hv >= 2:
                            lh = (hv - 2) * 4 + 3
                            W("sync", sP, pv_done[(lh * 2 + 1, 15)])
                        dma("sync", 3 + hv % 2, kbuf[hv % 2][:, :], k_sp[:, hv, :])
                        kv_load[hv] = dma("sync", 3 + hv % 2, vbuf[hv % 2][:, :],
                                          v_sp[:, :, hv * 128:(hv + 1) * 128])

                    load_kv(0)
                    wq_load[0] = dma("gpsimd", 5, wqb[0][:, :], wqt_d[0])
                    wq_load[1] = dma("gpsimd", 6, wqb[1][:, :], wqt_d[1])
                    WD("tensor", wq_load[0])
                    WD("tensor", c_xfm)
                    for i, lam in enumerate(q_mm_lams(0)):
                        q_done[0] = emit("tensor", lam, inc=(i == 31))
                    W("vector", sP, q_done[0])
                    qev[0] = emit("vector", lambda e: e.tensor_copy(Qb[0][:, :], pq[0][:, :]),
                                  inc=True)
                    emit_s03(0)

                    for h in range(32):
                        hv = h // 4
                        if h % 4 == 3 and hv + 1 < 8:
                            load_kv(hv + 1)
                        for qc in range(2):
                            g = h * 2 + qc
                            if qc == 1 and h + 2 < 32:
                                W("gpsimd", sP, q_done[2 * h + 1])
                                wq_load[h + 2] = dma("gpsimd", 5 + h % 2, wqb[h % 2][:, :],
                                                     wqt_d[h + 2])
                            nxt = q_mm_lams(g + 1) if g + 1 < 64 else None
                            nxt_waits = []
                            if nxt is not None:
                                nh = (g + 1) // 2
                                if nh != h:
                                    nxt_waits.append(("D", wq_load[nh]))
                                if g >= 1:
                                    nxt_waits.append(("V", bc_done[g - 1]))

                            def mk_d(kt, g=g):
                                return lambda e: e.matmul(
                                    pd[:, :], onec[:, :],
                                    Eb[(kt // 2) % 3][:, (kt % 2) * 512:(kt % 2 + 1) * 512],
                                    start=(kt == 0), stop=(kt == 15))

                            def mk_pv(kt, g=g, hv=hv):
                                return lambda e: e.matmul(
                                    ppv[:, :], vbuf[hv % 2][:, kt * 128:(kt + 1) * 128],
                                    Eb[(kt // 2) % 3][:, (kt % 2) * 512:(kt % 2 + 1) * 512],
                                    start=(kt == 0), stop=(kt == 15))

                            if g >= 1:
                                W("tensor", sV, recip_done[g - 1])
                                W("tensor", sV, norm_done[g - 1])
                            for kt2 in range(8):
                                W("scalar", sP, s_done[(g, kt2 * 2 + 1)])
                                P = g * 8 + kt2 - 3
                                if P >= 0:
                                    gp, ktp = divmod(P, 8)
                                    W("scalar", sP, pv_done[(gp, ktp * 2 + 1)])
                                exp_done[(g, kt2)] = emit(
                                    "scalar",
                                    lambda e, kt2=kt2: e.activation(
                                        Eb[kt2 % 3][:, :], ps[kt2 % 2][:, :],
                                        AF.Exp, scale=SCALE), inc=True)
                                W("tensor", sA, exp_done[(g, kt2)])
                                for half in range(2):
                                    kt = kt2 * 2 + half
                                    r = emit("tensor", mk_d(kt), inc=(kt == 15))
                                    if kt == 15:
                                        d15[g] = r
                                    pv_done[(g, kt)] = emit("tensor", mk_pv(kt), inc=True)
                                if nxt is not None:
                                    for w in nxt_waits:
                                        if w[0] == "D":
                                            WD("tensor", w[1])
                                        else:
                                            W("tensor", sV, w[1])
                                    nxt_waits = []
                                    for i in range(4):
                                        lam = nxt[kt2 * 4 + i]
                                        q_done[g + 1] = emit("tensor", lam,
                                                             inc=(kt2 == 7 and i == 3))
                                if kt2 < 6:
                                    kt = kt2 * 2 + 4
                                    s_done[(g, kt)] = emit("tensor", mk_s(kt, g, hv),
                                                           inc=False)
                                    s_done[(g, kt + 1)] = emit("tensor", mk_s(kt + 1, g, hv),
                                                               inc=True)
                            if nxt is not None:
                                W("vector", sP, q_done[g + 1])
                                qev[g + 1] = emit(
                                    "vector",
                                    lambda e, g=g: e.tensor_copy(Qb[(g + 1) % 2][:, :],
                                                                 pq[(g + 1) % 2][:, :]),
                                    inc=True)
                            W("vector", sP, d15[g])
                            recip_done[g] = emit("vector", lambda e:
                                                 e.reciprocal(rden[:, :], pd[:, :]), inc=True)
                            if nxt is not None:
                                emit_s03(g + 1)
                            W("tensor", sV, recip_done[g])
                            bcmm = emit("tensor", lambda e, g=g:
                                        e.matmul(pq[g % 2][:, :], oner[:, :], rden[:, :],
                                                 start=True, stop=True), inc=True)
                            W("vector", sP, bcmm)
                            bc_done[g] = emit("vector", lambda e, g=g:
                                              e.tensor_copy(rdbc[:, :], pq[g % 2][:, :]),
                                              inc=True)
                            W("vector", sP, pv_done[(g, 15)])
                            norm_done[g] = emit(
                                "vector",
                                lambda e, h=h, qc=qc:
                                e.tensor_tensor(attn[:, h, qc * 512:(qc + 1) * 512],
                                                ppv[:, :], rdbc[:, :], ALU.mult), inc=True)
                    drain(); b.flush()

                with ExitStack() as esB2:
                    atm = esB2.enter_context(nc.sbuf_tensor(f"atmB{H}", [128, 8, DM], BF16))
                    esB2i = esB2.enter_context(ExitStack())
                    wob = [esB2i.enter_context(nc.sbuf_tensor(f"woB{H}_{i}", [128, DM], BF16)) for i in range(2)]
                    woc = [esB2i.enter_context(nc.sbuf_tensor(f"wocB{H}_{i}", [128, 1024], BF16)) for i in range(2)]
                    pwo = [esB2i.enter_context(nc.psum_tensor(f"pwoB{H}_{i}", [128, 512], F32)) for i in range(2)]
                    ptb = [esB2i.enter_context(nc.psum_tensor(f"ptbB{H}_{i}", [128, 512], F32)) for i in range(2)]

                    woev = {}
                    tev = {}
                    tmm = {}
                    bi = 0
                    for oc in range(32):
                        if oc >= 2:
                            W("gpsimd", sP, tmm[(oc - 2, 1)])
                        c_wo = dma("gpsimd", 1 + oc % 2, wob[oc % 2][:, :], wot_d[oc])
                        WD("tensor", c_wo)
                        for tcol in range(2):
                            if bi >= 2:
                                W("tensor", sV, woev[bi - 2])
                            for dc in range(32):
                                r = emit(
                                    "tensor",
                                    lambda e, oc=oc, tcol=tcol, dc=dc:
                                    e.matmul(pwo[tcol % 2][:, :],
                                             wob[oc % 2][:, dc * 128:(dc + 1) * 128],
                                             attn[:, dc, tcol * 512:(tcol + 1) * 512],
                                             start=(dc == 0), stop=(dc == 31)),
                                    inc=(dc == 31))
                            W("vector", sP, r)
                            woev[bi] = emit(
                                "vector", lambda e, oc=oc, tcol=tcol:
                                e.tensor_copy(woc[oc % 2][:, tcol * 512:(tcol + 1) * 512],
                                              pwo[tcol % 2][:, :]), inc=True)
                            bi += 1
                        for tb in range(2):
                            gb = oc * 2 + tb
                            if gb >= 2:
                                W("tensor", sV, tev[gb - 2])
                            W("tensor", sV, woev[bi - 1])
                            for j in range(4):
                                ts = tb * 4 + j
                                r = emit(
                                    "tensor",
                                    lambda e, oc=oc, ts=ts, j=j, tb=tb:
                                    e.matmul(ptb[tb % 2][:, j * 128:(j + 1) * 128],
                                             woc[oc % 2][:, ts * 128:(ts + 1) * 128],
                                             ident[:, :], start=True, stop=True),
                                    inc=(j == 3))
                            tmm[(oc, tb)] = r
                            W("vector", sP, r)
                            tev[gb] = emit(
                                "vector",
                                lambda e, oc=oc, tb=tb:
                                e.tensor_copy(
                                    atm[:, tb * 4:(tb + 1) * 4, oc * 128:(oc + 1) * 128],
                                    ptb[tb % 2][:, :].rearrange("p (c t) -> p c t", c=4)),
                                inc=True)
                    drain(); b.flush()
                    esB2i.close()

                    with ExitStack() as esB3:
                        xres_ = esB3.enter_context(nc.sbuf_tensor(f"xrB{H}", [128, DM], F32))
                        xres = [xres_, xres_]
                        sres = [esB3.enter_context(nc.sbuf_tensor(f"srB{H}_{i}", [128, DM], F32)) for i in range(2)]
                        r1t = [esB3.enter_context(nc.sbuf_tensor(f"r1B{H}_{i}", [128, DM], BF16, side="right")) for i in range(2)]
                        st_ = esB3.enter_context(nc.sbuf_tensor(f"stB{H}", [128, 16], F32, side="right"))

                        addv = {}; nrm = {}; r1w = {}; sq_done = {}
                        for ts in range(8):
                            tg = H * 8 + ts
                            base = 8 * (ts % 2)
                            if ts >= 1:
                                W("sync", sV, addv[ts - 1])
                                W("sync", sA, sq_done[ts - 1])
                            c_x = dma("sync", 3, xres[ts % 2][:, :],
                                      x_d[tg * 128:(tg + 1) * 128, :])
                            WD("vector", c_x)
                            if ts >= 2:
                                W("vector", sV, nrm[ts - 2])
                            addv[ts] = emit(
                                "vector", lambda e, ts=ts:
                                e.tensor_tensor(sres[ts % 2][:, :], xres[ts % 2][:, :],
                                                atm[:, ts, :], ALU.add), inc=True)
                            W("scalar", sV, addv[ts])
                            sq_d = emit("scalar", lambda e, ts=ts, base=base:
                                        e.activation(xres[ts % 2][:, :], sres[ts % 2][:, :],
                                                     AF.Square,
                                                     accum_out=st_[:, base + 1:base + 2]),
                                        inc=True)
                            sq_done[ts] = sq_d
                            sum_d = emit("vector", lambda e, ts=ts, base=base:
                                         e.reduce_sum(st_[:, base:base + 1], sres[ts % 2][:, :],
                                                      axis=AXL.X), inc=True)
                            W("vector", sA, sq_d)
                            W("vector", sV, sum_d)
                            m1 = emit("vector", lambda e, base=base:
                                      e.tensor_scalar_mul(st_[:, base + 2:base + 3],
                                                          st_[:, base:base + 1], 1.0 / DM),
                                      inc=True)
                            m2 = emit("vector", lambda e, base=base:
                                      e.tensor_scalar_mul(st_[:, base + 3:base + 4],
                                                          st_[:, base + 1:base + 2], 1.0 / DM),
                                      inc=True)
                            W("vector", sV, m2)
                            m3 = emit("vector", lambda e, base=base:
                                      e.tensor_tensor(st_[:, base + 7:base + 8],
                                                      st_[:, base + 2:base + 3],
                                                      st_[:, base + 2:base + 3], ALU.mult),
                                      inc=True)
                            W("vector", sV, m3)
                            m4 = emit("vector", lambda e, base=base:
                                      e.scalar_tensor_tensor(st_[:, base + 4:base + 5],
                                                             st_[:, base + 3:base + 4], EPS,
                                                             st_[:, base + 7:base + 8],
                                                             ALU.add, ALU.subtract), inc=True)
                            W("scalar", sV, m4)
                            sqr = emit("scalar", lambda e, base=base:
                                       e.activation(st_[:, base + 5:base + 6],
                                                    st_[:, base + 4:base + 5], AF.Sqrt),
                                       inc=True)
                            W("vector", sA, sqr)
                            rcp = emit("vector", lambda e, base=base:
                                       e.reciprocal(st_[:, base + 6:base + 7],
                                                    st_[:, base + 5:base + 6]), inc=True)
                            W("vector", sV, rcp)
                            if ts >= 2:
                                WD("vector", r1w[ts - 2])
                            nrm[ts] = emit(
                                "vector", lambda e, ts=ts, base=base:
                                e.tensor_scalar(r1t[ts % 2][:, :], sres[ts % 2][:, :],
                                                st_[:, base + 2:base + 3],
                                                st_[:, base + 6:base + 7],
                                                ALU.subtract, ALU.mult), inc=True)
                            W("gpsimd", sV, nrm[ts])
                            r1w[ts] = dma("gpsimd", 5 + ts % 2, r1_sp[tg * 128:(tg + 1) * 128, :],
                                          r1t[ts % 2][:, :])
                        drain(); b.flush()

        if dump:
            c = dma("gpsimd", 0, dbg["r1"], r1_sp[:, :])
            drain()
            b.flush()
        if dump == 1:
            drain()
            b.flush()
            return nc

        with ExitStack() as esC:
            r1fm = esC.enter_context(nc.sbuf_tensor("r1fm", [128, 32, 2048], BF16, side="right"))

            with ExitStack() as esC0:
                r1tm = [esC0.enter_context(nc.sbuf_tensor(f"r1tmC{i}", [128, DM], BF16)) for i in range(2)]
                ptb = [esC0.enter_context(nc.psum_tensor(f"ptbC{i}", [128, 512], F32)) for i in range(2)]
                tmmc = {}; tevc = {}
                bi = 0
                for ts in range(16):
                    if ts >= 2:
                        W("sync", sP, tmmc[ts - 2])
                    c_r = dma("sync", 1 + ts % 2, r1tm[ts % 2][:, :], r1_sp[ts * 128:(ts + 1) * 128, :])
                    WD("tensor", c_r)
                    for cb in range(8):
                        if bi >= 2:
                            W("tensor", sV, tevc[bi - 2])
                        for j in range(4):
                            c = cb * 4 + j
                            r = emit(
                                "tensor",
                                lambda e, pb=ptb[bi % 2], ts=ts, c=c, j=j:
                                e.matmul(pb[:, j * 128:(j + 1) * 128],
                                         r1tm[ts % 2][:, c * 128:(c + 1) * 128],
                                         ident[:, :], start=True, stop=True),
                                inc=(j == 3))
                        tmmc[ts] = r
                        W("vector", sP, r)
                        tevc[bi] = emit(
                            "vector",
                            lambda e, pb=ptb[bi % 2], cb=cb, ts=ts:
                            e.tensor_copy(r1fm[:, cb * 4:(cb + 1) * 4, ts * 128:(ts + 1) * 128],
                                          pb[:, :].rearrange("p (c t) -> p c t", c=4)),
                            inc=True)
                        bi += 1
                drain(); b.flush()

            with ExitStack() as esC1:
                qq = {m: [esC1.enter_context(nc.sbuf_tensor(f"qq{m}{i}", [128, 1024], I32))
                          for i in range(2)] for m in range(2)}
                szt = {m: [esC1.enter_context(nc.sbuf_tensor(f"szt{m}{i}", [128, 64], F32))
                           for i in range(2)] for m in range(2)}
                w1 = {m: esC1.enter_context(nc.sbuf_tensor(f"w1C{m}", [128, DM], BF16))
                      for m in range(2)}
                wT = {m: esC1.enter_context(nc.sbuf_tensor(f"wTC{m}", [128, DM], BF16))
                      for m in range(2)}
                sgb = [esC1.enter_context(nc.sbuf_tensor(f"sgbC{i}", [128, 512], BF16)) for i in range(4)]
                hst = [esC1.enter_context(nc.sbuf_tensor(f"hstC{i}", [128, 512], BF16)) for i in range(4)]
                ptb = [esC1.enter_context(nc.psum_tensor(f"ptbC1{i}", [128, 512], F32)) for i in range(2)]
                pg = [esC1.enter_context(nc.psum_tensor(f"pgC{i}", [128, 512], F32)) for i in range(2)]
                pu = [esC1.enter_context(nc.psum_tensor(f"puC{i}", [128, 512], F32)) for i in range(4)]

                qsrc = {0: qg_d, 1: qu_d}
                ssrc = {0: sg_d, 1: su_d}
                zsrc = {0: zg_d, 1: zu_d}
                qload = {}; szload = {}; deq = {}; mmt = {}; tevc = {}
                mmg = {}; silu_d = {}; hmul = {}; hw = {}
                HWCH = {0: 7, 1: 8, 2: 9, 3: 0}

                def emit_hmul(ic):
                    rows_ = slice(ic * 128, (ic + 1) * 128)
                    for tcol in range(4):
                        W("vector", sP, mmg[(ic, 1, tcol)])
                        W("vector", sA, silu_d[(ic, tcol)])
                        if (ic - 1, tcol) in hw:
                            WD("vector", hw[(ic - 1, tcol)])
                        hmul[(ic, tcol)] = emit(
                            "vector",
                            lambda e, tcol=tcol:
                            e.tensor_tensor(hst[tcol][:, :], pu[tcol][:, :],
                                            sgb[tcol][:, :], ALU.mult), inc=True)
                        W("gpsimd", sV, hmul[(ic, tcol)])
                        hw[(ic, tcol)] = dma("gpsimd", HWCH[tcol],
                                             h_sp[rows_, tcol * 512:(tcol + 1) * 512],
                                             hst[tcol][:, :])

                bti = 0
                for ic in range(NIC):
                    rows = slice(ic * 128, (ic + 1) * 128)
                    for m in range(2):
                        if ic >= 2:
                            W("sync", sV, deq[(ic - 2, m, 3)])
                        dma("sync", 5 + m, szt[m][ic % 2][:, 0:32], ssrc[m][rows, :])
                        szload[(ic, m)] = dma("sync", 5 + m, szt[m][ic % 2][:, 32:64],
                                              zsrc[m][rows, :])
                        for j in range(2):
                            if ic >= 1:
                                W("sync", sV, deq[(ic - 1, m, j + 2)])
                            qload[(ic, m, j)] = dma(
                                "sync", 1 + m * 2 + j % 2, qq[m][j % 2][:, :],
                                qsrc[m][rows, j * 1024:(j + 1) * 1024])
                    for m in range(2):
                        if ic >= 1:
                            W("vector", sP, mmt[(ic - 1, m)])
                        WD("vector", szload[(ic, m)])
                        for j in range(4):
                            WD("vector", qload[(ic, m, j)])
                            for g8 in range(8):
                                g = j * 8 + g8
                                deq[(ic, m, j)] = emit(
                                    "vector",
                                    lambda e, m=m, j=j, g=g, g8=g8, ic=ic:
                                    e.tensor_scalar(w1[m][:, g * 128:(g + 1) * 128],
                                                    qq[m][j % 2][:, g8 * 128:(g8 + 1) * 128],
                                                    szt[m][ic % 2][:, 32 + g:33 + g],
                                                    szt[m][ic % 2][:, g:g + 1],
                                                    ALU.subtract, ALU.mult),
                                    inc=(g8 == 7))
                            if j + 2 <= 3:
                                W("sync", sV, deq[(ic, m, j)])
                                qload[(ic, m, j + 2)] = dma(
                                    "sync", 1 + m * 2 + j % 2, qq[m][j % 2][:, :],
                                    qsrc[m][rows, (j + 2) * 1024:(j + 3) * 1024])
                    for m in range(2):
                        if ic >= 1:
                            W("vector", sP, mmg[(ic - 1, m, 3)])
                        for cb in range(8):
                            if bti >= 2:
                                W("tensor", sV, tevc[bti - 2])
                            W("tensor", sV, deq[(ic, m, cb // 2)])
                            for j in range(4):
                                g = cb * 4 + j
                                r = emit(
                                    "tensor",
                                    lambda e, pb=ptb[bti % 2], m=m, g=g, j=j:
                                    e.matmul(pb[:, j * 128:(j + 1) * 128],
                                             w1[m][:, g * 128:(g + 1) * 128],
                                             ident[:, :], start=True, stop=True),
                                    inc=(j == 3))
                            W("vector", sP, r)
                            tevc[bti] = emit(
                                "vector",
                                lambda e, pb=ptb[bti % 2], m=m, cb=cb:
                                e.tensor_copy(wT[m][:, cb * 512:(cb + 1) * 512], pb[:, :]),
                                inc=True)
                            bti += 1
                        mmt[(ic, m)] = r
                    wT_ready = tevc[bti - 1]
                    if ic >= 1:
                        emit_hmul(ic - 1)
                    for tcol in range(4):
                        for m in range(2):
                            pp = pg[tcol % 2] if m == 0 else pu[tcol]
                            W("tensor", sV, wT_ready)
                            if m == 0:
                                prev = silu_d.get((ic, tcol - 2), silu_d.get((ic - 1, tcol + 2)))
                                if prev:
                                    W("tensor", sA, prev)
                            else:
                                if (ic - 1, tcol) in hmul:
                                    W("tensor", sV, hmul[(ic - 1, tcol)])
                            for kc in range(32):
                                r = emit(
                                    "tensor",
                                    lambda e, pp=pp, m=m, kc=kc, tcol=tcol:
                                    e.matmul(pp[:, :],
                                             wT[m][:, kc * 128:(kc + 1) * 128],
                                             r1fm[:, kc, tcol * 512:(tcol + 1) * 512],
                                             start=(kc == 0), stop=(kc == 31)),
                                    inc=(kc == 31))
                            mmg[(ic, m, tcol)] = r
                        W("scalar", sP, mmg[(ic, 0, tcol)])
                        if (ic - 1, tcol) in hmul:
                            W("scalar", sV, hmul[(ic - 1, tcol)])
                        silu_d[(ic, tcol)] = emit(
                            "scalar",
                            lambda e, tcol=tcol: e.activation(sgb[tcol][:, :],
                                                              pg[tcol % 2][:, :], AF.Silu),
                            inc=True)
                emit_hmul(NIC - 1)
                drain(); b.flush()

        with ExitStack() as esC2:
            sdt = esC2.enter_context(nc.sbuf_tensor("sdtC", [128, 32, NIC], F32))
            zdt = esC2.enter_context(nc.sbuf_tensor("zdtC", [128, 32, NIC], F32))
            qsl = [esC2.enter_context(nc.sbuf_tensor(f"qslC{i}", [128, DM], I32)) for i in range(2)]
            w1d = [esC2.enter_context(nc.sbuf_tensor(f"w1dC{i}", [128, DM], BF16)) for i in range(2)]
            stg = [esC2.enter_context(nc.sbuf_tensor(f"stgC{i}", [128, DM], BF16)) for i in range(2)]
            ptb = [esC2.enter_context(nc.psum_tensor(f"ptbC2{i}", [128, 512], F32)) for i in range(2)]

            dma("sync", 5, sdt[:, :, :], sdt_d)
            c_sz = dma("sync", 5, zdt[:, :, :], zdt_d)
            WD("vector", c_sz)
            ql = {}; dq = {}; mt = {}; tev2 = {}; stw = {}
            bti = 0
            for gi in range(NIC):
                if gi >= 2:
                    W("sync", sV, dq[gi - 2])
                ql[gi] = dma("sync", 1 + gi % 2, qsl[gi % 2][:, :], qdt_d[gi])
                WD("vector", ql[gi])
                if gi >= 2:
                    W("vector", sP, mt[gi - 2])
                for hoc in range(32):
                    dq[gi] = emit(
                        "vector",
                        lambda e, gi=gi, hoc=hoc:
                        e.tensor_scalar(w1d[gi % 2][:, hoc * 128:(hoc + 1) * 128],
                                        qsl[gi % 2][:, hoc * 128:(hoc + 1) * 128],
                                        zdt[:, hoc, gi:gi + 1], sdt[:, hoc, gi:gi + 1],
                                        ALU.subtract, ALU.mult),
                        inc=(hoc == 31))
                W("tensor", sV, dq[gi])
                for cb in range(8):
                    if bti >= 2:
                        W("tensor", sA, tev2[bti - 2])
                    for j in range(4):
                        hoc = cb * 4 + j
                        r = emit(
                            "tensor",
                            lambda e, pb=ptb[bti % 2], gi=gi, hoc=hoc, j=j:
                            e.matmul(pb[:, j * 128:(j + 1) * 128],
                                     w1d[gi % 2][:, hoc * 128:(hoc + 1) * 128],
                                     ident[:, :], start=True, stop=True),
                            inc=(j == 3))
                    mt[gi] = r
                    W("scalar", sP, r)
                    if cb == 0 and gi >= 2:
                        WD("scalar", stw[gi - 2])
                    tev2[bti] = emit(
                        "scalar",
                        lambda e, pb=ptb[bti % 2], gi=gi, cb=cb:
                        e.copy(stg[gi % 2][:, cb * 512:(cb + 1) * 512], pb[:, :]),
                        inc=True)
                    bti += 1
                W("gpsimd", sA, tev2[bti - 1])
                stw[gi] = dma("gpsimd", 3 + gi % 2, wdt_sp[gi * 128:(gi + 1) * 128, :], stg[gi % 2][:, :])
            drain(); b.flush()

        if dump == 2:
            dma("gpsimd", 0, dbg["h"], h_sp[:, 0:512])
            dma("gpsimd", 0, dbg["wdt"], wdt_sp[0:512, :])
            drain()
            b.flush()
            return nc

        with ExitStack() as esD:
            ht = esD.enter_context(nc.sbuf_tensor("htD", [128, NIC, 512], BF16))
            ftm = esD.enter_context(nc.sbuf_tensor("ftmD", [128, 4, DM], F32))
            wdb = [esD.enter_context(nc.sbuf_tensor(f"wdbD{i}", [128, 1024], BF16)) for i in range(4)]
            r1q = esD.enter_context(nc.sbuf_tensor("r1qD", [128, 4, 1024], BF16))
            obuf = esD.enter_context(nc.sbuf_tensor("obufD", [128, DM], F32))
            st_ = esD.enter_context(nc.sbuf_tensor("stD", [128, 24], F32))
            pf = [esD.enter_context(nc.psum_tensor(f"pfD{i}", [128, 1024], F32)) for i in range(4)]

            WDCH = {0: 2, 1: 3, 2: 6, 3: 7}
            hload = {}; dmm = {}; fev = {}; r1l = {}; outw = {}
            for tm in range(4):
                tc = slice(tm * 512, (tm + 1) * 512)

                def issue_h(icl, tm=tm, tc=tc):
                    if tm >= 1:
                        W("gpsimd", sP, dmm[(tm - 1, 3, icl)])
                    hload[(tm, icl)] = dma("gpsimd", 1, ht[:, icl, :],
                                           h_sp[icl * 128:(icl + 1) * 128, tc])

                for hoq in range(4):
                    hc = slice(hoq * 1024, (hoq + 1) * 1024)
                    for ts in range(4):
                        prevf = fev.get((tm, hoq - 1, ts), fev.get((tm - 1, 3, ts)))
                        if prevf:
                            W("sync", sV, prevf)
                        r1l[(tm, hoq, ts)] = dma(
                            "sync", 4 + ts // 2, r1q[:, ts, :],
                            r1_sp[tm * 512 + ts * 128: tm * 512 + (ts + 1) * 128, hc])
                    for ic in range(NIC):
                        par = ic % 4
                        gmm = dmm.get((tm, hoq, ic - 4), dmm.get((tm, hoq - 1, NIC - 4 + ic),
                                                                 dmm.get((tm - 1, 3, NIC - 4 + ic))))
                        if gmm:
                            W("sync", sP, gmm)
                        c_w = dma("sync", WDCH[par], wdb[par][:, :],
                                  wdt_sp[ic * 128:(ic + 1) * 128, hc])
                        WD("tensor", c_w)
                        if hoq == 0:
                            if tm == 0:
                                if ic == 0:
                                    for k in range(24):
                                        issue_h(k)
                                elif ic + 23 < NIC:
                                    issue_h(ic + 23)
                                WD("tensor", hload[(tm, min(ic + 23, NIC - 1))])
                            else:
                                if ic + 23 < NIC and ic + 23 >= 24:
                                    issue_h(ic + 23)
                                WD("tensor", hload[(tm, min(max(ic + 23, 23), NIC - 1))])
                        if ic == 0:
                            for ts in range(4):
                                prevf = fev.get((tm, hoq - 1, ts), fev.get((tm - 1, 3, ts)))
                                if prevf:
                                    W("tensor", sV, prevf)
                        for ts in range(4):
                            for nn in range(2):
                                r = emit(
                                    "tensor",
                                    lambda e, par=par, ts=ts, nn=nn, ic=ic:
                                    e.matmul(pf[ts][:, nn * 512:(nn + 1) * 512],
                                             ht[:, ic, ts * 128:(ts + 1) * 128],
                                             wdb[par][:, nn * 512:(nn + 1) * 512],
                                             start=(ic == 0), stop=(ic == NIC - 1)),
                                    inc=(ts == 3 and nn == 1))
                        dmm[(tm, hoq, ic)] = r
                    for ts in range(4):
                        W("vector", sP, dmm[(tm, hoq, NIC - 1)])
                        WD("vector", r1l[(tm, hoq, (ts // 2) * 2 + 1)])
                        fev[(tm, hoq, ts)] = emit(
                            "vector",
                            lambda e, ts=ts, hoq=hoq:
                            e.tensor_tensor(ftm[:, ts, hoq * 1024:(hoq + 1) * 1024],
                                            pf[ts][:, :], r1q[:, ts, :], ALU.add), inc=True)
                if tm + 1 < 4:
                    tc_n = slice((tm + 1) * 512, (tm + 2) * 512)
                    for icl in range(24):
                        W("gpsimd", sP, dmm[(tm, 3, icl)])
                        hload[(tm + 1, icl)] = dma("gpsimd", 1, ht[:, icl, :],
                                                   h_sp[icl * 128:(icl + 1) * 128, tc_n])
                for ts in range(4):
                    tg = tm * 4 + ts
                    base = 8 * (ts % 2)
                    W("scalar", sV, fev[(tm, 3, ts)])
                    if tg >= 1:
                        WD("scalar", outw[tg - 1])
                    sq_d = emit("scalar", lambda e, ts=ts, base=base:
                                e.activation(obuf[:, :], ftm[:, ts, :], AF.Square,
                                             accum_out=st_[:, base + 1:base + 2]), inc=True)
                    W("vector", sV, fev[(tm, 3, ts)])
                    sum_d = emit("vector", lambda e, ts=ts, base=base:
                                 e.reduce_sum(st_[:, base:base + 1], ftm[:, ts, :],
                                              axis=AXL.X), inc=True)
                    W("vector", sA, sq_d)
                    W("vector", sV, sum_d)
                    m1 = emit("vector", lambda e, base=base:
                              e.tensor_scalar_mul(st_[:, base + 2:base + 3],
                                                  st_[:, base:base + 1], 1.0 / DM), inc=True)
                    m2 = emit("vector", lambda e, base=base:
                              e.tensor_scalar_mul(st_[:, base + 3:base + 4],
                                                  st_[:, base + 1:base + 2], 1.0 / DM), inc=True)
                    W("vector", sV, m2)
                    m3 = emit("vector", lambda e, base=base:
                              e.tensor_tensor(st_[:, base + 7:base + 8],
                                              st_[:, base + 2:base + 3],
                                              st_[:, base + 2:base + 3], ALU.mult), inc=True)
                    W("vector", sV, m3)
                    m4 = emit("vector", lambda e, base=base:
                              e.scalar_tensor_tensor(st_[:, base + 4:base + 5],
                                                     st_[:, base + 3:base + 4], EPS,
                                                     st_[:, base + 7:base + 8],
                                                     ALU.add, ALU.subtract), inc=True)
                    W("scalar", sV, m4)
                    sqr = emit("scalar", lambda e, base=base:
                               e.activation(st_[:, base + 5:base + 6],
                                            st_[:, base + 4:base + 5], AF.Sqrt), inc=True)
                    W("vector", sA, sqr)
                    rcp = emit("vector", lambda e, base=base:
                               e.reciprocal(st_[:, base + 6:base + 7],
                                            st_[:, base + 5:base + 6]), inc=True)
                    W("vector", sV, rcp)
                    W("vector", sA, sq_d)
                    if tg >= 1:
                        WD("vector", outw[tg - 1])
                    nrm = emit("vector", lambda e, ts=ts, base=base:
                               e.tensor_scalar(obuf[:, :], ftm[:, ts, :],
                                               st_[:, base + 2:base + 3],
                                               st_[:, base + 6:base + 7],
                                               ALU.subtract, ALU.mult), inc=True)
                    W("gpsimd", sV, nrm)
                    outw[tg] = dma("gpsimd", 8, out_d[tg * 128:(tg + 1) * 128, :], obuf[:, :])
            drain()
            b.flush()

    return nc


def _host_prep(inputs):
    Wq = np.asarray(inputs["L0_attn_Wq"])
    Wk = np.asarray(inputs["L0_attn_Wk"])
    Wv = np.asarray(inputs["L0_attn_Wv"])
    Wo = np.asarray(inputs["L0_attn_Wo"])
    qd = np.asarray(inputs["L0_ffn_qw_down"])
    sd = np.asarray(inputs["L0_ffn_s_down"])
    zd = np.asarray(inputs["L0_ffn_z_down"])
    common = {
        "wqt_d": np.ascontiguousarray(Wq.T.reshape(32, 128, 32, 128).transpose(2, 1, 0, 3)),
        "wkt_d": np.ascontiguousarray(Wk.T.reshape(32, 128, 1024).transpose(1, 0, 2)),
        "wvt_d": np.ascontiguousarray(Wv.T.reshape(32, 128, 1024).transpose(1, 0, 2)),
        "wot_d": np.ascontiguousarray(Wo.T.reshape(32, 128, 32, 128).transpose(2, 1, 0, 3)),
        "qg_d": np.ascontiguousarray(inputs["L0_ffn_qw_gate"]),
        "qu_d": np.ascontiguousarray(inputs["L0_ffn_qw_up"]),
        "sg_d": np.ascontiguousarray(inputs["L0_ffn_s_gate"]),
        "zg_d": np.ascontiguousarray(inputs["L0_ffn_z_gate"]),
        "su_d": np.ascontiguousarray(inputs["L0_ffn_s_up"]),
        "zu_d": np.ascontiguousarray(inputs["L0_ffn_z_up"]),
        "qdt_d": np.ascontiguousarray(qd.reshape(32, 128, 86, 128).transpose(2, 1, 0, 3)),
        "sdt_d": np.ascontiguousarray(sd.reshape(32, 128, 86).transpose(1, 0, 2)),
        "zdt_d": np.ascontiguousarray(zd.reshape(32, 128, 86).transpose(1, 0, 2)),
        "ident_d": np.eye(128, dtype=bfloat16),
        "onec_d": np.ones((128, 1), dtype=bfloat16),
        "oner_d": np.ones((1, 128), dtype=np.float32),
    }
    x = np.asarray(inputs["L0_attn_x"])
    maps = []
    for core in range(8):
        m = dict(common)
        m["x_d"] = np.ascontiguousarray(x[core])
        maps.append(m)
    return maps


def kernel(**inputs):
    maps = _host_prep(inputs)
    nc = build_nc(dump=0)
    res = run_bass_kernel_spmd(nc, maps, core_ids=list(range(8)))
    out = np.stack([np.asarray(res.results[c]["out_d"]) for c in range(8)], axis=0)
    return out.astype(np.float32)
